# version 03
# baseline (speedup 1.0000x reference)
import os
import numpy as np
from contextlib import ExitStack

import concourse.bass as bass
import concourse.mybir as mybir
from concourse.bass_utils import run_bass_kernel_spmd

F32 = mybir.dt.float32
BF16 = mybir.dt.bfloat16
AF = mybir.ActivationFunctionType
ALU = mybir.AluOpType
AX = mybir.AxisListType

NCORES = 8
TOK = 2048
NT = 16
D = 2048
KV = 512
NH = 16
NKV = 4
F = 768
NFC = 6
EPS = 1e-8
H_FFN = 1408
NHC = 11
NE = 34
NDC = 16
HALF = 1024
NTH = 8
BIG = 1e30


class _SelfSem:

    def __init__(self, semh):
        self.s = semh
        self.n = 0

    def inc(self, instr):
        instr.then_inc(self.s, 1)
        self.n += 1
        return instr

    def bar(self, eng):
        eng.wait_ge(self.s, self.n)


def _make_identity(g, identity, semh, base):
    sq = identity.shape[0]
    g.memset(identity, 0.0).then_inc(semh, 1)
    g.wait_ge(semh, base + 1)
    g.affine_select(out=identity, in_=identity, compare_op=ALU.not_equal,
                    fill=1.0, base=0, pattern=[[-1, sq]], channel_multiplier=1,
                    ).then_inc(semh, 1)
    return base + 2


def build_nc():
    nc = bass.Bass("TRN2", target_bir_lowering=False,
                   detect_race_conditions=False)

    x_in = nc.dram_tensor("x", [TOK, D], F32, kind="ExternalInput").ap()
    wkT = nc.dram_tensor("wkT", [D, KV], F32, kind="ExternalInput").ap()
    wvT = nc.dram_tensor("wvT", [D, KV], F32, kind="ExternalInput").ap()
    wqT = nc.dram_tensor("wqT", [D, D], F32, kind="ExternalInput").ap()
    wgT = nc.dram_tensor("wgT", [D, D], F32, kind="ExternalInput").ap()
    wmbT = nc.dram_tensor("wmbT", [D, D], F32, kind="ExternalInput").ap()
    bk = nc.dram_tensor("bk", [KV], F32, kind="ExternalInput").ap()
    bv = nc.dram_tensor("bv", [KV], F32, kind="ExternalInput").ap()
    bq = nc.dram_tensor("bq", [D], F32, kind="ExternalInput").ap()
    bg = nc.dram_tensor("bg", [D], F32, kind="ExternalInput").ap()
    bmb = nc.dram_tensor("bmb", [D], F32, kind="ExternalInput").ap()
    mem_in = nc.dram_tensor("mem", [NH, F, 128], F32, kind="ExternalInput").ap()
    nrm_in = nc.dram_tensor("nrm", [NH, F], F32, kind="ExternalInput").ap()
    w1h = nc.dram_tensor("w1h", [NE, NHC, 128, NDC * 128], F32,
                         kind="ExternalInput").ap()
    w2h = nc.dram_tensor("w2h", [NE, H_FFN, D], F32, kind="ExternalInput").ap()
    b1h = nc.dram_tensor("b1h", [NE, H_FFN], F32, kind="ExternalInput").ap()
    b2h = nc.dram_tensor("b2h", [NE, D], F32, kind="ExternalInput").ap()
    rwr = nc.dram_tensor("rwr", [D, 32], F32, kind="ExternalInput").ap()
    rbr = nc.dram_tensor("rbr", [32], F32, kind="ExternalInput").ap()
    y_out = nc.dram_tensor("y", [TOK, D], F32, kind="ExternalOutput").ap()

    xT_sp = nc.dram_tensor("xT_sp", [NT, 128, D], BF16).ap()
    mkb_sp = nc.dram_tensor("mkb_sp", [NT, 128, NKV * F], BF16).ap()
    mv_sp = nc.dram_tensor("mv_sp", [NT, 128, D], F32).ap()
    mh_sp = nc.dram_tensor("mh_sp", [NT, 128, D], F32).ap()
    x1_sp = nc.dram_tensor("x1_sp", [NT, 128, D], F32).ap()
    cc_in = nc.dram_tensor("cc_in", [128, NKV * NFC * 516], F32).ap()
    cc_out = nc.dram_tensor("cc_out", [128, NKV * NFC * 516], F32).ap()

    ctx = ExitStack()
    with ctx:
        def sem(name):
            return ctx.enter_context(nc.semaphore(name))

        s_ph = sem("s_ph")
        VS = _SelfSem(sem("s_vrt"))
        s_p0 = sem("s_p0")
        s_ini = sem("s_ini")
        s_p0t = sem("s_p0t")
        s_p0d = sem("s_p0d")
        s_xb0 = sem("s_xb0")
        s_xb1 = sem("s_xb1")
        s_tr = sem("s_tr")
        s_trd = sem("s_trd")
        s_kvp = sem("s_kvp")
        s_kvd = sem("s_kvd")
        s_f = sem("s_f")
        s_ss = sem("s_ss")
        s_sq = sem("s_sq")
        s_mkb = sem("s_mkb")
        s_mksq = sem("s_mksq")
        s_nump = sem("s_nump")
        s_numd = sem("s_numd")
        s_tdone = sem("s_tdone")
        s_xts = sem("s_xts")
        s_mks = sem("s_mks")
        s_mvs = sem("s_mvs")
        s_wmb = sem("s_wmb")
        s_mk2a = sem("s_mk2a")
        s_mk2b = sem("s_mk2b")
        s_xT2l = sem("s_xT2l")
        s_mv2l = sem("s_mv2l")
        s_mbp = sem("s_mbp")
        s_mbd = sem("s_mbd")
        s_wmv = sem("s_wmv")
        s_dmp = sem("s_dmp")
        s_dmd = sem("s_dmd")
        s_ar = sem("s_ar")
        s_arg = sem("s_arg")
        s_cc = sem("s_cc")
        s_nm = sem("s_nm")
        s_wql = sem("s_wql")
        s_xT3l = sem("s_xT3l")
        s_qp = sem("s_qp")
        s_qd = sem("s_qd")
        s_bf = sem("s_bf")
        s_bss = sem("s_bss")
        s_bsq = sem("s_bsq")
        s_bmk = sem("s_bmk")
        s_btr = sem("s_btr")
        s_btrd = sem("s_btrd")
        s_nap = sem("s_nap")
        s_nad = sem("s_nad")
        s_mhs = sem("s_mhs")
        s_wgl = sem("s_wgl")
        s_xT4l = sem("s_xT4l")
        s_mh2l = sem("s_mh2l")
        s_xf2a = sem("s_xf2a")
        s_xf2b = sem("s_xf2b")
        s_gp = sem("s_gp")
        s_gd = sem("s_gd")
        s_r1 = sem("s_r1")
        s_sq2 = sem("s_sq2")
        s_sqr = sem("s_sqr")
        s_x1s = sem("s_x1s")
        s_cl = sem("s_cl")
        s_ctr = sem("s_ctr")
        s_ctrd = sem("s_ctrd")
        s_lgp = sem("s_lgp")
        s_sm1 = sem("s_sm1")
        s_sm2 = sem("s_sm2")
        s_lgd = sem("s_lgd")
        s_gtp = sem("s_gtp")
        s_gtd = sem("s_gtd")
        s_w1s = [sem("s_w1s0"), sem("s_w1s1"), sem("s_w1s2")]
        s_w2 = sem("s_w2")
        s_y1p = sem("s_y1p")
        s_y1d = sem("s_y1d")
        s_y2p = sem("s_y2p")
        s_y2d = sem("s_y2d")
        s_ep = sem("s_ep")
        s_ep2 = sem("s_ep2")
        s_ep3 = sem("s_ep3")
        s_osa = sem("s_osa")
        s_osb = sem("s_osb")

        ident = ctx.enter_context(nc.sbuf_tensor("ident", [128, 128], BF16))
        ones = ctx.enter_context(nc.sbuf_tensor("ones", [1, 128], BF16))
        rbrrow = ctx.enter_context(nc.sbuf_tensor("rbrrow", [1, 32], BF16))
        b1col = ctx.enter_context(nc.sbuf_tensor("b1col", [128, NHC, NE], F32))
        b2raw = ctx.enter_context(nc.sbuf_tensor("b2raw", [NE, D], BF16))
        rwsb = ctx.enter_context(nc.sbuf_tensor("rwsb", [128, NDC, 32], BF16))
        epsb = ctx.enter_context(nc.sbuf_tensor("epsb", [128, 1], F32))

        block = ctx.enter_context(nc.Block())

        PH = {p: 5 * i for i, p in enumerate(
            ["P0", "A1", "A2", "AR", "B1", "B2", "C"])}

        def _barrier(next_val, waits=None):
            waits = waits or {}
            for name, emitter in (("gpsimd", block.gpsimd), ("tensor", block.tensor),
                                  ("vector", block.vector), ("scalar", block.scalar),
                                  ("sync", block.sync)):
                def _body(e, nv=next_val, ws=waits.get(name, [])):
                    for semh, cnt in ws:
                        e.wait_ge(semh, cnt)
                    e.nop().then_inc(s_ph, 1)
                    e.wait_ge(s_ph, nv)
                emitter(_body)

        attn_ctx = ExitStack()

        def asb(*a):
            return attn_ctx.enter_context(nc.sbuf_tensor(*a))

        identF = asb("identF", [128, 128], F32)
        normT = asb("normT", [128, NFC, NH], F32)
        coef = asb("coef", [128, NT, NH], F32)
        bkrow = asb("bkrow", [1, KV], BF16)
        bvrow = asb("bvrow", [1, KV], BF16)
        bqrow = asb("bqrow", [1, D], BF16)
        bgrow = asb("bgrow", [1, D], BF16)
        bmbrow = asb("bmbrow", [1, D], BF16)
        nrmsb = asb("nrmsb", [NH, F], F32)
        b1bf = asb("b1bf", [NE, H_FFN], BF16)

        sA = ExitStack()

        def a1sb(*a):
            return sA.enter_context(nc.sbuf_tensor(*a))

        def a1ps(*a):
            return sA.enter_context(nc.psum_tensor(*a))

        wk_sb = a1sb("wk_sb", [128, NDC, KV], BF16)
        wv_sb = a1sb("wv_sb", [128, NDC, KV], BF16)
        mext = a1sb("mext", [128, NH, NFC, 129], BF16)
        xb = a1sb("xb", [128, 2, D], BF16)
        xT = a1sb("xT", [128, NDC, 128], BF16)
        kf = a1sb("kf", [128, KV], F32)
        vf = a1sb("vf", [128, KV], F32)
        rr = a1sb("rr", [128, 256], F32)
        fbuf = a1sb("fbuf", [128, F], F32)
        fsq = a1sb("fsq", [128, F], F32)
        mkb = a1sb("mkb", [128, NKV, F], BF16)
        mkT = a1sb("mkT", [128, NKV, NFC, 128], BF16)
        mvb = a1sb("mvb", [128, D], F32)
        ssb = a1sb("ssb", [128, NKV], F32)
        sqb = a1sb("sqb", [128, NKV], F32)
        rsb = a1sb("rsb", [128, NKV], F32)
        mksq = a1sb("mksq", [128, NKV], F32)
        rmksq = a1sb("rmksq", [128, NKV], F32)
        dnE = a1sb("dnE", [128, NH], F32)
        rdnn = a1sb("rdnn", [128, NH], F32)
        ctmp = a1sb("ctmp", [128, NH], F32)
        ptr = a1ps("ptr", [128, 2, 1024], BF16)
        ptrF = a1ps("ptrF", [128, 512], F32)
        pkv = a1ps("pkv", [128, 2, 512], F32)
        pnum = a1ps("pnum", [128, 3, 512], F32)

        P0_G_INIT = 6
        P0_DMAS = 6 + 1 + 1 + 1 + 1 + NH + 1 + 1
        P0_G_TOTAL = 16 * P0_DMAS

        @block.gpsimd
        def _(g):
            base = _make_identity(g, ident[:], s_ini, 0)
            base = _make_identity(g, identF[:], s_ini, base)
            g.memset(ones[:], 1.0).then_inc(s_ini, 1)
            g.memset(epsb[:], 1e-24).then_inc(s_ini, 1)
            g.dma_start(out=bkrow[:], in_=bk[None, :]).then_inc(s_p0, 16)
            g.dma_start(out=bvrow[:], in_=bv[None, :]).then_inc(s_p0, 16)
            g.dma_start(out=bqrow[:], in_=bq[None, :]).then_inc(s_p0, 16)
            g.dma_start(out=bgrow[:], in_=bg[None, :]).then_inc(s_p0, 16)
            g.dma_start(out=bmbrow[:], in_=bmb[None, :]).then_inc(s_p0, 16)
            g.dma_start(out=rbrrow[:], in_=rbr[None, :]).then_inc(s_p0, 16)
            g.dma_start(out=nrmsb[:], in_=nrm_in[:, :]).then_inc(s_p0, 16)
            g.dma_start(out=b1bf[:], in_=b1h[:, :]).then_inc(s_p0, 16)
            g.dma_start(out=b2raw[:], in_=b2h[:, :]).then_inc(s_p0, 16)
            g.dma_start(out=rwsb[:], in_=rwr.rearrange("(dc p) e -> p dc e", p=128)
                        ).then_inc(s_p0, 16)
            for h in range(NH):
                g.dma_start(out=mext[:, h, :, 0:128],
                            in_=mem_in[h].rearrange("(fc p) d -> p fc d", p=128)
                            ).then_inc(s_p0, 16)
            g.dma_start(out=wk_sb[:], in_=wkT.rearrange("(dc p) o -> p dc o", p=128)
                        ).then_inc(s_p0, 16)
            g.dma_start(out=wv_sb[:], in_=wvT.rearrange("(dc p) o -> p dc o", p=128)
                        ).then_inc(s_p0, 16)

        @block.tensor
        def _(t):
            t.wait_ge(s_ini, P0_G_INIT)
            t.wait_ge(s_p0, P0_G_TOTAL)
            for fc in range(NFC):
                if fc >= 1:
                    t.wait_ge(s_p0d, fc)
                t.transpose(ptrF[:, 0:NH], nrmsb[:, fc * 128:(fc + 1) * 128],
                            identF[0:NH, 0:NH]).then_inc(s_p0t, 1)
            for hc in range(NHC):
                if hc >= 2:
                    t.wait_ge(s_p0d, NFC + hc - 1)
                t.transpose(ptr[:, hc % 2, 0:NE],
                            b1bf[:, hc * 128:(hc + 1) * 128], ident[0:NE, 0:NE]
                            ).then_inc(s_p0t, 1)

        @block.vector
        def _(v):
            for fc in range(NFC):
                v.wait_ge(s_p0t, fc + 1)
                v.tensor_copy(normT[:, fc, :], ptrF[:, 0:NH]).then_inc(s_p0d, 1)
            for hc in range(NHC):
                v.wait_ge(s_p0t, NFC + hc + 1)
                v.tensor_copy(b1col[:, hc, :], ptr[:, hc % 2, 0:NE]).then_inc(s_p0d, 1)
            v.wait_ge(s_p0, P0_G_TOTAL)
            v.tensor_copy(mext[:, :, :, 128],
                          normT.rearrange("p fc h -> p h fc")).then_inc(s_p0d, 1)

        _barrier(PH["A1"], {
            "gpsimd": [(s_p0, 16 * P0_DMAS), (s_ini, P0_G_INIT)],
            "tensor": [(s_p0t, NFC + NHC)],
            "vector": [(s_p0d, NFC + NHC + 1)],
        })

        @block.gpsimd
        def _(g):
            for i in range(NT):
                if i >= 2:
                    g.wait_ge(s_tr, 40 * (i - 1))
                g.dma_start(out=xb[:, i % 2], in_=x_in[i * 128:(i + 1) * 128, :]
                            ).then_inc(s_xb0 if i % 2 == 0 else s_xb1, 16)

        @block.tensor
        def _(t):
            for i in range(NT):
                t.wait_ge(s_xb0 if i % 2 == 0 else s_xb1, 16 * (i // 2 + 1))
                t.wait_ge(s_xts, 16 * i)
                for j in range(16):
                    n = 40 * i + j
                    if n >= 2:
                        t.wait_ge(s_trd, n - 1)
                    t.transpose(ptr[:, n % 2, 0:128],
                                xb[:, i % 2, j * 128:(j + 1) * 128],
                                ident[:]).then_inc(s_tr, 1)
                t.wait_ge(s_trd, 40 * i + 16)
                t.wait_ge(s_kvd, 2 * i)
                for dc in range(NDC):
                    t.matmul(pkv[:, 0], xT[:, dc, :], wk_sb[:, dc, :],
                             start=(dc == 0), stop=False)
                t.matmul(pkv[:, 0], ones[:, 0:128], bkrow[:],
                         start=False, stop=True).then_inc(s_kvp, 1)
                for dc in range(NDC):
                    t.matmul(pkv[:, 1], xT[:, dc, :], wv_sb[:, dc, :],
                             start=(dc == 0), stop=False)
                t.matmul(pkv[:, 1], ones[:, 0:128], bvrow[:],
                         start=False, stop=True).then_inc(s_kvp, 1)
                for kv in range(NKV):
                    t.wait_ge(s_mkb, 4 * i + kv + 1)
                    for fc in range(NFC):
                        n = 40 * i + 16 + kv * NFC + fc
                        t.wait_ge(s_trd, n - 1)
                        t.transpose(ptr[:, n % 2, 0:128],
                                    mkb[:, kv, fc * 128:(fc + 1) * 128],
                                    ident[:]).then_inc(s_tr, 1)
                for h in range(NH):
                    kv = h // 4
                    t.wait_ge(s_trd, 40 * i + 16 + NFC * (kv + 1))
                    if 16 * i + h >= 3:
                        t.wait_ge(s_numd, 16 * i + h - 2)
                    for fc in range(NFC):
                        mm = t.matmul(pnum[:, h % 3, 0:129], mkT[:, kv, fc, :],
                                      mext[:, h, fc, :],
                                      start=(fc == 0), stop=(fc == NFC - 1))
                    mm.then_inc(s_nump, 1)

        @block.scalar
        def _(s):
            for i in range(NT):
                s.wait_ge(s_tdone, i)
                s.wait_ge(s_kvp, 2 * i + 1)
                s.copy(kf[:], pkv[:, 0]).then_inc(s_kvd, 1)
                s.wait_ge(s_kvp, 2 * i + 2)
                s.copy(vf[:], pkv[:, 1]).then_inc(s_kvd, 1)
                for kv in range(NKV):
                    s.wait_ge(s_f, 6 * (4 * i + kv + 1))
                    s.activation(fsq[:], fbuf[:], AF.Square,
                                 accum_out=ssb[:, kv:kv + 1]).then_inc(s_ss, 1)
                    s.wait_ge(s_ss, 4 * i + kv + 1)
                    s.activation(sqb[:, kv:kv + 1], ssb[:, kv:kv + 1], AF.Sqrt,
                                 bias=epsb[:]).then_inc(s_sq, 1)
                    s.wait_ge(s_mkb, 4 * i + kv + 1)
                    s.activation(fsq[:], mkb[:, kv, :], AF.Square,
                                 accum_out=mksq[:, kv:kv + 1]).then_inc(s_mksq, 1)

        @block.vector
        def _(v):
            for i in range(NT):
                v.wait_ge(s_kvp, 2 * i)
                v.wait_ge(s_xts, 16 * i)
                for j in range(16):
                    n = 40 * i + j
                    v.wait_ge(s_tr, n + 1)
                    v.tensor_copy(xT[:, j, :], ptr[:, n % 2, 0:128]).then_inc(s_trd, 1)
                v.wait_ge(s_tr, 40 * i)
                v.wait_ge(s_mks, 16 * i)
                v.wait_ge(s_mksq, 4 * i)
                v.wait_ge(s_nump, 16 * i)
                for kv in range(NKV):
                    v.wait_ge(s_kvd, 2 * i + 1)
                    if 4 * i + kv >= 1:
                        v.wait_ge(s_ss, 4 * i + kv)
                    kcol = kf[:, kv * 128:(kv + 1) * 128]
                    VS.inc(v.tensor_scalar_max(rr[:, 0:128], kcol, 0.0))
                    VS.inc(v.tensor_scalar(rr[:, 128:256], kcol, -1.0, 0.0,
                                           op0=ALU.mult, op1=ALU.max))
                    VS.bar(v)
                    for jj in range(3):
                        sft = jj + 1
                        base = jj * 256
                        v.tensor_tensor(fbuf[:, base + sft:base + 256],
                                        rr[:, sft:256], rr[:, 0:256 - sft],
                                        op=ALU.mult).then_inc(s_f, 1)
                        v.tensor_tensor(fbuf[:, base:base + sft],
                                        rr[:, 0:sft], rr[:, 256 - sft:256],
                                        op=ALU.mult).then_inc(s_f, 1)
                    v.wait_ge(s_f, 6 * (4 * i + kv + 1))
                    v.wait_ge(s_sq, 4 * i + kv + 1)
                    VS.inc(v.reciprocal(rsb[:, kv:kv + 1], sqb[:, kv:kv + 1]))
                    VS.bar(v)
                    v.tensor_scalar(mkb[:, kv, :], fbuf[:], rsb[:, kv:kv + 1], None,
                                    op0=ALU.mult).then_inc(s_mkb, 1)
                    for fc in range(NFC):
                        n = 40 * i + 16 + kv * NFC + fc
                        v.wait_ge(s_tr, n + 1)
                        v.tensor_copy(mkT[:, kv, fc, :], ptr[:, n % 2, 0:128]
                                      ).then_inc(s_trd, 1)
                v.wait_ge(s_mvs, 16 * i)
                for h in range(NH):
                    kv = h // 4
                    v.wait_ge(s_nump, 16 * i + h + 1)
                    VS.inc(v.tensor_scalar_add(dnE[:, h:h + 1],
                                               pnum[:, h % 3, 128:129], EPS))
                    VS.bar(v)
                    VS.inc(v.reciprocal(rdnn[:, h:h + 1], dnE[:, h:h + 1]))
                    VS.bar(v)
                    VS.inc(v.tensor_scalar_mul(rdnn[:, h:h + 1], rdnn[:, h:h + 1],
                                               -1.0))
                    VS.bar(v)
                    v.scalar_tensor_tensor(mvb[:, h * 128:(h + 1) * 128],
                                           pnum[:, h % 3, 0:128], rdnn[:, h:h + 1],
                                           vf[:, kv * 128:(kv + 1) * 128],
                                           op0=ALU.mult, op1=ALU.add
                                           ).then_inc(s_numd, 1)
                v.wait_ge(s_mksq, 4 * (i + 1))
                VS.inc(v.reciprocal(rmksq[:], mksq[:]))
                VS.bar(v)
                for kv in range(NKV):
                    VS.inc(v.tensor_scalar(ctmp[:, 4 * kv:4 * kv + 4],
                                           dnE[:, 4 * kv:4 * kv + 4],
                                           rmksq[:, kv:kv + 1],
                                           None, op0=ALU.mult))
                VS.bar(v)
                VS.inc(v.tensor_scalar(ctmp[:], ctmp[:], -1.0, 1.0,
                                       op0=ALU.mult, op1=ALU.add))
                VS.bar(v)
                v.tensor_scalar(coef[:, i, :], ctmp[:], 0.0, 1.0,
                                op0=ALU.max, op1=ALU.min).then_inc(s_tdone, 1)

        @block.sync
        def _(sy):
            for i in range(NT):
                sy.wait_ge(s_trd, 40 * i + 16)
                sy.dma_start(out=xT_sp[i].rearrange("p (dc t) -> p dc t", dc=NDC),
                             in_=xT[:]).then_inc(s_xts, 16)
                sy.wait_ge(s_mkb, 4 * (i + 1))
                sy.dma_start(out=mkb_sp[i].rearrange("p (kv f) -> p kv f", kv=NKV),
                             in_=mkb[:]).then_inc(s_mks, 16)
                sy.wait_ge(s_numd, 16 * (i + 1))
                sy.wait_ge(s_tdone, i + 1)
                sy.dma_start(out=mv_sp[i], in_=mvb[:]).then_inc(s_mvs, 16)

        sA.close()
        _barrier(PH["A2"], {
            "gpsimd": [(s_xb0, 16 * 8), (s_xb1, 16 * 8)],
            "tensor": [(s_nump, 16 * NT), (s_kvp, 2 * NT), (s_tr, 40 * NT)],
            "scalar": [(s_mksq, 4 * NT), (s_kvd, 2 * NT), (s_sq, 4 * NT)],
            "vector": [(s_tdone, NT), (s_numd, 16 * NT), (s_trd, 40 * NT),
                       (s_mkb, 4 * NT), (s_f, 24 * NT), (VS.s, VS.n)],
            "sync": [(s_xts, 16 * NT), (s_mks, 16 * NT), (s_mvs, 16 * NT)],
        })

        sA2 = ExitStack()
        dmem = sA2.enter_context(nc.sbuf_tensor("dmem", [128, NKV, NFC, 516], F32))
        sA2a = ExitStack()
        wmb_sb = sA2a.enter_context(nc.sbuf_tensor("wmb_sb", [128, NDC, D], BF16))
        mkb2 = sA2a.enter_context(nc.sbuf_tensor("mkb2", [128, 2, NKV, F], BF16))
        xT2 = sA2a.enter_context(nc.sbuf_tensor("xT2", [128, NDC, 128], BF16))
        mv2 = sA2a.enter_context(nc.sbuf_tensor("mv2", [128, D], F32))
        mbf = sA2a.enter_context(nc.sbuf_tensor("mbf", [128, D], F32))
        wmv2 = sA2a.enter_context(nc.sbuf_tensor("wmv2", [128, 2, NKV, 516], BF16))
        pmb = sA2a.enter_context(nc.psum_tensor("pmb", [128, 2, 512], F32))
        pdm = sA2a.enter_context(nc.psum_tensor("pdm", [128, 2, 512], F32))
        pdn = sA2a.enter_context(nc.psum_tensor("pdn", [128, 2, 512], F32))

        @block.gpsimd
        def _(g):
            g.dma_start(out=wmb_sb[:], in_=wmbT.rearrange("(dc p) o -> p dc o", p=128)
                        ).then_inc(s_wmb, 16)
            for i in range(NT):
                gi = i // 2
                if i >= 2:
                    g.wait_ge(s_dmp, 24 * gi)
                if i >= 1:
                    g.wait_ge(s_mbp, 4 * i)
                    g.wait_ge(s_wmv, 8 * i)
                g.dma_start(out=mkb2[:, i % 2],
                            in_=mkb_sp[i].rearrange("p (kv f) -> p kv f", kv=NKV)
                            ).then_inc(s_mk2a if i % 2 == 0 else s_mk2b, 16)
                g.dma_start(out=xT2[:], in_=xT_sp[i].rearrange(
                    "p (dc t) -> p dc t", dc=NDC)).then_inc(s_xT2l, 16)
                g.dma_start(out=mv2[:], in_=mv_sp[i]).then_inc(s_mv2l, 16)

        @block.tensor
        def _(t):
            for i in range(NT):
                gi, ti = i // 2, i % 2
                t.wait_ge(s_wmb, 16)
                t.wait_ge(s_xT2l, 16 * (i + 1))
                for sl in range(4):
                    m = 4 * i + sl
                    if m >= 2:
                        t.wait_ge(s_mbd, m - 1)
                    for dc in range(NDC):
                        t.matmul(pmb[:, m % 2], xT2[:, dc, :],
                                 wmb_sb[:, dc, sl * 512:(sl + 1) * 512],
                                 start=(dc == 0), stop=False)
                    t.matmul(pmb[:, m % 2], ones[:, 0:128],
                             bmbrow[:, sl * 512:(sl + 1) * 512],
                             start=False, stop=True).then_inc(s_mbp, 1)
                if ti == 1:
                    t.wait_ge(s_wmv, 8 * (i + 1))
                    t.wait_ge(s_mk2a, 16 * (gi + 1))
                    t.wait_ge(s_mk2b, 16 * (gi + 1))
                    for kv in range(NKV):
                        for fc in range(NFC):
                            n = 24 * gi + kv * NFC + fc
                            if n >= 2:
                                t.wait_ge(s_dmd, 2 * (n - 1))
                            for tt in range(2):
                                t.matmul(pdm[:, n % 2],
                                         mkb2[:, tt, kv, fc * 128:(fc + 1) * 128],
                                         wmv2[:, tt, kv, 0:512],
                                         start=(tt == 0), stop=(tt == 1))
                                mm = t.matmul(pdn[:, n % 2, 0:4],
                                              mkb2[:, tt, kv, fc * 128:(fc + 1) * 128],
                                              wmv2[:, tt, kv, 512:516],
                                              start=(tt == 0), stop=(tt == 1))
                            mm.then_inc(s_dmp, 1)

        @block.scalar
        def _(s):
            for i in range(NT):
                for sl in range(4):
                    m = 4 * i + sl
                    s.wait_ge(s_mbp, m + 1)
                    if i >= 1:
                        s.wait_ge(s_wmv, 8 * i)
                    s.activation(mbf[:, sl * 512:(sl + 1) * 512], pmb[:, m % 2],
                                 AF.Sigmoid).then_inc(s_mbd, 1)

        @block.vector
        def _(v):
            v.memset(dmem[:], 0.0)
            for i in range(NT):
                gi, ti = i // 2, i % 2
                v.wait_ge(s_mbd, 4 * (i + 1))
                v.wait_ge(s_mv2l, 16 * (i + 1))
                if i >= 2:
                    v.wait_ge(s_dmp, 24 * gi)
                for kv in range(NKV):
                    v.tensor_tensor(wmv2[:, ti, kv, 0:512],
                                    mv2[:, kv * 512:(kv + 1) * 512],
                                    mbf[:, kv * 512:(kv + 1) * 512],
                                    op=ALU.mult).then_inc(s_wmv, 1)
                    v.tensor_copy(wmv2[:, ti, kv, 512:516],
                                  coef[:, i, 4 * kv:4 * kv + 4]).then_inc(s_wmv, 1)
                if ti == 1:
                    for kv in range(NKV):
                        for fc in range(NFC):
                            n = 24 * gi + kv * NFC + fc
                            v.wait_ge(s_dmp, n + 1)
                            v.tensor_tensor(dmem[:, kv, fc, 0:512],
                                            dmem[:, kv, fc, 0:512], pdm[:, n % 2],
                                            op=ALU.add).then_inc(s_dmd, 1)
                            v.tensor_tensor(dmem[:, kv, fc, 512:516],
                                            dmem[:, kv, fc, 512:516],
                                            pdn[:, n % 2, 0:4],
                                            op=ALU.add).then_inc(s_dmd, 1)

        sA2a.close()
        _barrier(PH["AR"], {
            "gpsimd": [(s_wmb, 16), (s_mk2a, 16 * 8), (s_mk2b, 16 * 8),
                       (s_xT2l, 16 * NT), (s_mv2l, 16 * NT)],
            "tensor": [(s_dmp, 24 * 8), (s_mbp, 4 * NT)],
            "scalar": [(s_mbd, 4 * NT)],
            "vector": [(s_dmd, 2 * 24 * 8), (s_wmv, 8 * NT), (VS.s, VS.n)],
        })

        sAR = ExitStack()
        nmem = sAR.enter_context(nc.sbuf_tensor("nmem", [128, NH, NFC, 129], BF16))
        sARm = ExitStack()
        memf = sARm.enter_context(nc.sbuf_tensor("memf", [128, NH, NFC, 128], F32))
        NM_TOT = NH * NFC

        @block.sync
        def _(sy):
            sy.dma_start(out=cc_in[:, :],
                         in_=dmem.rearrange("p kv fc c -> p (kv fc c)")
                         ).then_inc(s_ar, 16)

        @block.gpsimd
        def _(g):
            g.dma_start(out=memf[:],
                        in_=mem_in.rearrange("h (fc p) d -> p h fc d", p=128)
                        ).then_inc(s_arg, 16)
            g.wait_ge(s_ar, 16)
            g.wait_ge(s_arg, 16)
            g.collective_compute(
                "AllReduce", ALU.add,
                replica_groups=[list(range(NCORES))],
                ins=[cc_in[:, :].opt()], outs=[cc_out[:, :].opt()],
            ).then_inc(s_cc, 1)
            g.wait_ge(s_cc, 1)
            g.dma_start(out=dmem.rearrange("p kv fc c -> p (kv fc c)"),
                        in_=cc_out[:, :]).then_inc(s_arg, 16)

        @block.vector
        def _(v):
            v.wait_ge(s_arg, 32)
            v.wait_ge(s_cc, 1)
            for h in range(NH):
                kv, j = h // 4, h % 4
                for fc in range(NFC):
                    v.tensor_tensor(nmem[:, h, fc, 0:128], memf[:, h, fc, :],
                                    dmem[:, kv, fc, j * 128:(j + 1) * 128],
                                    op=ALU.add).then_inc(s_nm, 1)
                    v.tensor_tensor(nmem[:, h, fc, 128:129], normT[:, fc, h:h + 1],
                                    dmem[:, kv, fc, 512 + j:513 + j],
                                    op=ALU.add).then_inc(s_nm, 1)

        sARm.close()
        _barrier(PH["B1"], {
            "gpsimd": [(s_arg, 32), (s_cc, 1)],
            "sync": [(s_ar, 16)],
            "vector": [(s_nm, 2 * NH * NFC), (VS.s, VS.n)],
        })

        sB1 = ExitStack()
        wq_sb = sB1.enter_context(nc.sbuf_tensor("wq_sb", [128, NDC, D], BF16))
        xT3 = sB1.enter_context(nc.sbuf_tensor("xT3", [128, NDC, 128], BF16))
        qf = sB1.enter_context(nc.sbuf_tensor("qf", [128, D], F32))
        rr3 = sB1.enter_context(nc.sbuf_tensor("rr3", [128, 256], F32))
        fb3 = sB1.enter_context(nc.sbuf_tensor("fb3", [128, F], F32))
        fs3 = sB1.enter_context(nc.sbuf_tensor("fs3", [128, F], F32))
        mqb = sB1.enter_context(nc.sbuf_tensor("mqb", [128, F], BF16))
        mqT = sB1.enter_context(nc.sbuf_tensor("mqT", [128, NFC, 128], BF16))
        mhb = sB1.enter_context(nc.sbuf_tensor("mhb", [128, D], F32))
        bssq = sB1.enter_context(nc.sbuf_tensor("bssq", [128, 2], F32))
        bsqr = sB1.enter_context(nc.sbuf_tensor("bsqr", [128, 2], F32))
        brs = sB1.enter_context(nc.sbuf_tensor("brs", [128, 2], F32))
        bdna = sB1.enter_context(nc.sbuf_tensor("bdna", [128, 2], F32))
        ptr3 = sB1.enter_context(nc.psum_tensor("ptr3", [128, 2, 1024], BF16))
        pq = sB1.enter_context(nc.psum_tensor("pq", [128, 2, 512], F32))
        pna = sB1.enter_context(nc.psum_tensor("pna", [128, 3, 512], F32))

        @block.gpsimd
        def _(g):
            g.dma_start(out=wq_sb[:], in_=wqT.rearrange("(dc p) o -> p dc o", p=128)
                        ).then_inc(s_wql, 16)
            for i in range(NT):
                if i >= 1:
                    g.wait_ge(s_qp, 4 * i)
                g.dma_start(out=xT3[:], in_=xT_sp[i].rearrange(
                    "p (dc t) -> p dc t", dc=NDC)).then_inc(s_xT3l, 16)

        @block.tensor
        def _(t):
            for i in range(NT):
                t.wait_ge(s_wql, 16)
                t.wait_ge(s_xT3l, 16 * (i + 1))
                for sl in range(4):
                    m = 4 * i + sl
                    if m >= 2:
                        t.wait_ge(s_qd, m - 1)
                    for dc in range(NDC):
                        t.matmul(pq[:, m % 2], xT3[:, dc, :],
                                 wq_sb[:, dc, sl * 512:(sl + 1) * 512],
                                 start=(dc == 0), stop=False)
                    t.matmul(pq[:, m % 2], ones[:, 0:128],
                             bqrow[:, sl * 512:(sl + 1) * 512],
                             start=False, stop=True).then_inc(s_qp, 1)
                for h in range(NH):
                    hh = NH * i + h
                    t.wait_ge(s_bmk, hh + 1)
                    for fc in range(NFC):
                        n = 6 * hh + fc
                        if n >= 2:
                            t.wait_ge(s_btrd, n - 1)
                        t.transpose(ptr3[:, n % 2, 0:128],
                                    mqb[:, fc * 128:(fc + 1) * 128],
                                    ident[:]).then_inc(s_btr, 1)
                    t.wait_ge(s_btrd, 6 * (hh + 1))
                    if hh >= 3:
                        t.wait_ge(s_nad, hh - 2)
                    for fc in range(NFC):
                        mm = t.matmul(pna[:, hh % 3, 0:129], mqT[:, fc, :],
                                      nmem[:, h, fc, :],
                                      start=(fc == 0), stop=(fc == NFC - 1))
                    mm.then_inc(s_nap, 1)

        @block.scalar
        def _(s):
            for i in range(NT):
                for sl in range(4):
                    m = 4 * i + sl
                    s.wait_ge(s_qp, m + 1)
                    if i >= 1:
                        s.wait_ge(s_bf, 6 * NH * i)
                    s.copy(qf[:, sl * 512:(sl + 1) * 512], pq[:, m % 2]
                           ).then_inc(s_qd, 1)
                for h in range(NH):
                    hh = NH * i + h
                    s.wait_ge(s_bf, 6 * (hh + 1))
                    if hh >= 2:
                        s.wait_ge(s_bmk, hh - 1)
                    s.activation(fs3[:], fb3[:], AF.Square,
                                 accum_out=bssq[:, hh % 2:hh % 2 + 1]).then_inc(s_bss, 1)
                    s.wait_ge(s_bss, hh + 1)
                    if hh >= 2:
                        s.wait_ge(s_bmk, hh - 1)
                    s.activation(bsqr[:, hh % 2:hh % 2 + 1], bssq[:, hh % 2:hh % 2 + 1],
                                 AF.Sqrt, bias=epsb[:]).then_inc(s_bsq, 1)

        @block.vector
        def _(v):
            for i in range(NT):
                v.wait_ge(s_mhs, 16 * i)
                for h in range(NH):
                    hh = NH * i + h
                    v.wait_ge(s_qd, 4 * i + (h // 4) + 1)
                    if hh >= 1:
                        v.wait_ge(s_bss, hh)
                        v.wait_ge(s_btr, 6 * hh)
                    qcol = qf[:, h * 128:(h + 1) * 128]
                    VS.inc(v.tensor_scalar_max(rr3[:, 0:128], qcol, 0.0))
                    VS.inc(v.tensor_scalar(rr3[:, 128:256], qcol, -1.0, 0.0,
                                           op0=ALU.mult, op1=ALU.max))
                    VS.bar(v)
                    for jj in range(3):
                        sft = jj + 1
                        base = jj * 256
                        v.tensor_tensor(fb3[:, base + sft:base + 256],
                                        rr3[:, sft:256], rr3[:, 0:256 - sft],
                                        op=ALU.mult).then_inc(s_bf, 1)
                        v.tensor_tensor(fb3[:, base:base + sft],
                                        rr3[:, 0:sft], rr3[:, 256 - sft:256],
                                        op=ALU.mult).then_inc(s_bf, 1)
                    v.wait_ge(s_bf, 6 * (hh + 1))
                    v.wait_ge(s_bsq, hh + 1)
                    VS.inc(v.reciprocal(brs[:, hh % 2:hh % 2 + 1],
                                        bsqr[:, hh % 2:hh % 2 + 1]))
                    VS.bar(v)
                    v.tensor_scalar(mqb[:], fb3[:], brs[:, hh % 2:hh % 2 + 1], None,
                                    op0=ALU.mult).then_inc(s_bmk, 1)
                    for fc in range(NFC):
                        n = 6 * hh + fc
                        v.wait_ge(s_btr, n + 1)
                        v.tensor_copy(mqT[:, fc, :], ptr3[:, n % 2, 0:128]
                                      ).then_inc(s_btrd, 1)
                    v.wait_ge(s_nap, hh + 1)
                    VS.inc(v.tensor_scalar_add(bdna[:, hh % 2:hh % 2 + 1],
                                               pna[:, hh % 3, 128:129], EPS))
                    VS.bar(v)
                    VS.inc(v.reciprocal(bdna[:, hh % 2:hh % 2 + 1],
                                        bdna[:, hh % 2:hh % 2 + 1]))
                    VS.bar(v)
                    v.tensor_scalar(mhb[:, h * 128:(h + 1) * 128],
                                    pna[:, hh % 3, 0:128],
                                    bdna[:, hh % 2:hh % 2 + 1], None,
                                    op0=ALU.mult).then_inc(s_nad, 1)

        @block.sync
        def _(sy):
            for i in range(NT):
                sy.wait_ge(s_nad, NH * (i + 1))
                sy.dma_start(out=mh_sp[i], in_=mhb[:]).then_inc(s_mhs, 16)

        sB1.close()
        sAR.close()
        sA2.close()
        _barrier(PH["B2"], {
            "gpsimd": [(s_wql, 16), (s_xT3l, 16 * NT)],
            "tensor": [(s_nap, NH * NT), (s_qp, 4 * NT), (s_btr, 6 * NH * NT)],
            "scalar": [(s_bsq, NH * NT), (s_qd, 4 * NT), (s_bss, NH * NT)],
            "vector": [(s_nad, NH * NT), (s_bmk, NH * NT), (s_btrd, 6 * NH * NT),
                       (s_bf, 6 * NH * NT), (VS.s, VS.n)],
            "sync": [(s_mhs, 16 * NT)],
        })

        sB2 = ExitStack()
        wg_sb = sB2.enter_context(nc.sbuf_tensor("wg_sb", [128, NDC, D], BF16))
        xf2 = sB2.enter_context(nc.sbuf_tensor("xf2", [128, 2, D], F32))
        xT4 = sB2.enter_context(nc.sbuf_tensor("xT4", [128, NDC, 128], BF16))
        mh2 = sB2.enter_context(nc.sbuf_tensor("mh2", [128, D], F32))
        gtf = sB2.enter_context(nc.sbuf_tensor("gtf", [128, D], F32))
        res2 = sB2.enter_context(nc.sbuf_tensor("res2", [128, D], F32))
        sqs2 = sB2.enter_context(nc.sbuf_tensor("sqs2", [128, D], F32))
        x1b2 = sB2.enter_context(nc.sbuf_tensor("x1b2", [128, D], F32))
        ln2 = sB2.enter_context(nc.sbuf_tensor("ln2", [128, 4], F32))
        pg = sB2.enter_context(nc.psum_tensor("pg", [128, 2, 512], F32))
        R1 = 6

        @block.gpsimd
        def _(g):
            g.dma_start(out=wg_sb[:], in_=wgT.rearrange("(dc p) o -> p dc o", p=128)
                        ).then_inc(s_wgl, 16)
            for i in range(NT):
                if i >= 1:
                    g.wait_ge(s_gp, 4 * i)
                    g.wait_ge(s_r1, R1 * (i - 1) + 1)
                g.dma_start(out=xT4[:], in_=xT_sp[i].rearrange(
                    "p (dc t) -> p dc t", dc=NDC)).then_inc(s_xT4l, 16)
                g.dma_start(out=mh2[:], in_=mh_sp[i]).then_inc(s_mh2l, 16)

        @block.sync
        def _(sy):
            for i in range(NT):
                if i >= 2:
                    sy.wait_ge(s_r1, R1 * (i - 2) + 1)
                sy.dma_start(out=xf2[:, i % 2], in_=x_in[i * 128:(i + 1) * 128, :]
                             ).then_inc(s_xf2a if i % 2 == 0 else s_xf2b, 16)
                sy.wait_ge(s_r1, R1 * (i + 1))
                sy.dma_start(out=x1_sp[i], in_=x1b2[:]).then_inc(s_x1s, 16)

        @block.tensor
        def _(t):
            for i in range(NT):
                t.wait_ge(s_wgl, 16)
                t.wait_ge(s_xT4l, 16 * (i + 1))
                for sl in range(4):
                    m = 4 * i + sl
                    if m >= 2:
                        t.wait_ge(s_gd, m - 1)
                    for dc in range(NDC):
                        t.matmul(pg[:, m % 2], xT4[:, dc, :],
                                 wg_sb[:, dc, sl * 512:(sl + 1) * 512],
                                 start=(dc == 0), stop=False)
                    t.matmul(pg[:, m % 2], ones[:, 0:128],
                             bgrow[:, sl * 512:(sl + 1) * 512],
                             start=False, stop=True).then_inc(s_gp, 1)

        @block.scalar
        def _(s):
            for i in range(NT):
                for sl in range(4):
                    m = 4 * i + sl
                    s.wait_ge(s_gp, m + 1)
                    if i >= 1:
                        s.wait_ge(s_r1, R1 * (i - 1) + 1)
                    s.activation(gtf[:, sl * 512:(sl + 1) * 512], pg[:, m % 2],
                                 AF.Sigmoid).then_inc(s_gd, 1)
                s.wait_ge(s_r1, R1 * i + 4)
                s.activation(sqs2[:], res2[:], AF.Square,
                             accum_out=ln2[:, 1:2]).then_inc(s_sq2, 1)
                s.wait_ge(s_r1, R1 * i + 5)
                s.activation(ln2[:, 3:4], ln2[:, 2:3], AF.Sqrt).then_inc(s_sqr, 1)

        @block.vector
        def _(v):
            for i in range(NT):
                v.wait_ge(s_gd, 4 * (i + 1))
                v.wait_ge(s_mh2l, 16 * (i + 1))
                v.wait_ge(s_xf2a if i % 2 == 0 else s_xf2b, 16 * (i // 2 + 1))
                if i >= 1:
                    v.wait_ge(s_sq2, i)
                    v.wait_ge(s_x1s, 16 * i)
                VS.inc(v.tensor_tensor(res2[:], gtf[:], mh2[:], op=ALU.mult))
                VS.bar(v)
                v.scalar_tensor_tensor(res2[:], xf2[:, i % 2], 2.0, res2[:],
                                       op0=ALU.mult, op1=ALU.add).then_inc(s_r1, 1)
                v.wait_ge(s_r1, 6 * i + 1)
                v.reduce_sum(ln2[:, 0:1], res2[:], axis=AX.X).then_inc(s_r1, 1)
                v.wait_ge(s_r1, 6 * i + 2)
                v.tensor_scalar_mul(ln2[:, 0:1], ln2[:, 0:1], -1.0 / D).then_inc(s_r1, 1)
                v.wait_ge(s_r1, 6 * i + 3)
                v.tensor_scalar(res2[:], res2[:], ln2[:, 0:1], None,
                                op0=ALU.add).then_inc(s_r1, 1)
                v.wait_ge(s_sq2, i + 1)
                v.tensor_scalar(ln2[:, 2:3], ln2[:, 1:2], 1.0 / D, 1e-5,
                                op0=ALU.mult, op1=ALU.add).then_inc(s_r1, 1)
                v.wait_ge(s_r1, 6 * i + 5)
                v.wait_ge(s_sqr, i + 1)
                VS.inc(v.reciprocal(ln2[:, 3:4], ln2[:, 3:4]))
                VS.bar(v)
                v.tensor_scalar(x1b2[:], res2[:], ln2[:, 3:4], None,
                                op0=ALU.mult).then_inc(s_r1, 1)

        sB2.close()
        attn_ctx.close()
        _barrier(PH["C"], {
            "gpsimd": [(s_wgl, 16), (s_xT4l, 16 * NT), (s_mh2l, 16 * NT)],
            "tensor": [(s_gp, 4 * NT)],
            "scalar": [(s_sqr, NT), (s_gd, 4 * NT), (s_sq2, NT)],
            "vector": [(s_r1, 6 * NT), (VS.s, VS.n)],
            "sync": [(s_x1s, 16 * NT), (s_xf2a, 16 * 8), (s_xf2b, 16 * 8)],
        })

        sC = ExitStack()
        accC = sC.enter_context(nc.sbuf_tensor("accC", [128, NTH, D], F32))
        x1T = sC.enter_context(nc.sbuf_tensor("x1T", [128, NDC, HALF], BF16))
        y1fm = sC.enter_context(nc.sbuf_tensor("y1fm", [128, NHC, HALF], BF16))
        w2sb = sC.enter_context(nc.sbuf_tensor("w2sb", [128, NHC, D], BF16))
        w1ch = sC.enter_context(nc.sbuf_tensor("w1ch", [128, 3, NDC * 128], BF16))
        gsb = sC.enter_context(nc.sbuf_tensor("gsb", [128, NTH, NE], F32))
        gbf = sC.enter_context(nc.sbuf_tensor("gbf", [128, NE], BF16))
        gT = sC.enter_context(nc.sbuf_tensor("gT", [NE, HALF], BF16))
        x1bb = sC.enter_context(nc.sbuf_tensor("x1bb", [128, D], BF16))
        expv = sC.enter_context(nc.sbuf_tensor("expv", [128, 32], F32))
        wrkC = sC.enter_context(nc.sbuf_tensor("wrkC", [128, 32], F32))
        mskC = sC.enter_context(nc.sbuf_tensor("mskC", [128, 32], F32))
        smC = sC.enter_context(nc.sbuf_tensor("smC", [128, 4], F32))
        lnC = sC.enter_context(nc.sbuf_tensor("lnC", [128, 4], F32))
        x1f = sC.enter_context(nc.sbuf_tensor("x1f", [128, D], F32))
        resC = sC.enter_context(nc.sbuf_tensor("resC", [128, D], F32))
        ptrC = sC.enter_context(nc.psum_tensor("ptrC", [128, 2, 1024], BF16))
        py1 = sC.enter_context(nc.psum_tensor("py1", [128, 2, 512], F32))
        py2 = sC.enter_context(nc.psum_tensor("py2", [128, 4, 512], F32))

        Y1G = 2 * NHC
        Y2G = NTH * 4
        EXP_Y2 = (NE + 1) * Y2G
        CTR = 17
        EP3 = 7

        for half in range(2):
            t0 = half * NTH
            b_cl = half * NTH * 16
            b_ctr = half * NTH * CTR
            b_lg = half * NTH
            b_w1 = half * NE * NHC
            b_w2 = half * NE * 16
            b_y1 = half * NE * Y1G
            b_y2 = half * EXP_Y2
            b_ep = half * NTH
            b_os = half * NTH * 16

            @block.gpsimd
            def _(g, half=half, t0=t0, b_cl=b_cl, b_ctr=b_ctr, b_w1=b_w1,
                  b_w2=b_w2, b_y2=b_y2):
                for tt in range(NTH):
                    glob = half * NTH + tt
                    if glob >= 1:
                        g.wait_ge(s_ctr, CTR * glob - 1)
                    g.dma_start(out=x1bb[:], in_=x1_sp[t0 + tt]).then_inc(s_cl, 16)
                for e in range(NE):
                    if half == 1 or e >= 1:
                        g.wait_ge(s_y2p, b_y2 + Y2G * e)
                    g.dma_start(out=w2sb[:],
                                in_=w2h[e].rearrange("(hc p) d -> p hc d", p=128)
                                ).then_inc(s_w2, 16)
                    for hc in range(NHC):
                        n = b_w1 + e * NHC + hc
                        if n >= 3:
                            g.wait_ge(s_y1p, 2 * (n - 2))
                        g.dma_start(out=w1ch[:, n % 3], in_=w1h[e, hc]
                                    ).then_inc(s_w1s[n % 3], 16)

            @block.tensor
            def _(t, half=half, b_cl=b_cl, b_ctr=b_ctr, b_lg=b_lg, b_w1=b_w1,
                  b_w2=b_w2, b_y1=b_y1, b_y2=b_y2):
                for tt in range(NTH):
                    glob = half * NTH + tt
                    t.wait_ge(s_cl, b_cl + 16 * (tt + 1))
                    if half == 1 and tt == 0:
                        t.wait_ge(s_y1p, NE * Y1G)
                        t.wait_ge(s_y1d, NE * Y1G)
                    for dc in range(NDC):
                        n = b_ctr + CTR * tt + dc
                        if n >= 2:
                            t.wait_ge(s_ctrd, n - 1)
                        t.transpose(ptrC[:, n % 2, 0:128],
                                    x1bb[:, dc * 128:(dc + 1) * 128],
                                    ident[:]).then_inc(s_ctr, 1)
                    t.wait_ge(s_ctrd, b_ctr + CTR * tt + 16)
                    if glob >= 2:
                        t.wait_ge(s_sm2, glob - 1)
                    for dc in range(NDC):
                        t.matmul(py1[:, tt % 2, 0:32],
                                 x1T[:, dc, tt * 128:(tt + 1) * 128],
                                 rwsb[:, dc, :], start=(dc == 0), stop=False)
                    t.matmul(py1[:, tt % 2, 0:32], ones[:, 0:128], rbrrow[:],
                             start=False, stop=True).then_inc(s_lgp, 1)
                    t.wait_ge(s_lgd, b_lg + tt + 1)
                    n = b_ctr + CTR * tt + 16
                    if n >= 2:
                        t.wait_ge(s_ctrd, n - 1)
                    t.transpose(ptrC[0:NE, n % 2, 0:128], gbf[:], ident[:]
                                ).then_inc(s_ctr, 1)
                t.wait_ge(s_sm2, (half + 1) * NTH)
                for e in range(NE):
                    for hc in range(NHC):
                        n = b_w1 + e * NHC + hc
                        t.wait_ge(s_w1s[n % 3], 16 * (n // 3 + 1))
                        for tg in range(2):
                            k = 2 * n + tg
                            if k >= 2:
                                t.wait_ge(s_y1d, k - 1)
                            for dc in range(NDC):
                                mm = t.matmul(
                                    py1[:, k % 2],
                                    w1ch[:, n % 3, dc * 128:(dc + 1) * 128],
                                    x1T[:, dc, tg * 512:(tg + 1) * 512],
                                    start=(dc == 0), stop=(dc == NDC - 1))
                            mm.then_inc(s_y1p, 1)
                    t.wait_ge(s_w2, b_w2 + 16 * (e + 1))
                    t.wait_ge(s_y1d, b_y1 + Y1G * (e + 1))
                    for tc in range(NTH):
                        for ds in range(4):
                            m = b_y2 + e * Y2G + tc * 4 + ds
                            if m >= 4:
                                t.wait_ge(s_y2d, m - 3)
                            for hc in range(NHC):
                                mm = t.matmul(py2[:, m % 4],
                                              y1fm[:, hc, tc * 128:(tc + 1) * 128],
                                              w2sb[:, hc, ds * 512:(ds + 1) * 512],
                                              start=(hc == 0), stop=(hc == NHC - 1))
                            mm.then_inc(s_y2p, 1)
                t.wait_ge(s_ctrd, (half + 1) * NTH * CTR)
                for tc in range(NTH):
                    for ds in range(4):
                        m = b_y2 + NE * Y2G + tc * 4 + ds
                        t.wait_ge(s_y2d, m - 3)
                        t.matmul(py2[:, m % 4], gT[:, tc * 128:(tc + 1) * 128],
                                 b2raw[:, ds * 512:(ds + 1) * 512],
                                 start=True, stop=True).then_inc(s_y2p, 1)

            @block.scalar
            def _(s, half=half, b_lg=b_lg, b_y1=b_y1, b_y2=b_y2, b_ep=b_ep):
                for tt in range(NTH):
                    glob = half * NTH + tt
                    s.wait_ge(s_sm1, glob + 1)
                    if glob >= 1:
                        s.wait_ge(s_lgd, glob)
                    s.activation(expv[:], py1[:, tt % 2, 0:32], AF.Exp,
                                 bias=smC[:, 0:1]).then_inc(s_sm2, 1)
                for e in range(NE):
                    for hc in range(NHC):
                        for tg in range(2):
                            k = b_y1 + e * Y1G + hc * 2 + tg
                            s.wait_ge(s_y1p, k + 1)
                            if e >= 1:
                                s.wait_ge(s_y2p, b_y2 + Y2G * e)
                            s.activation(y1fm[:, hc, tg * 512:(tg + 1) * 512],
                                         py1[:, k % 2], AF.Relu,
                                         bias=b1col[:, hc, e:e + 1]).then_inc(s_y1d, 1)
                for tt in range(NTH):
                    glob = half * NTH + tt
                    s.wait_ge(s_ep3, EP3 * glob + 4)
                    s.activation(x1f[:], resC[:], AF.Square,
                                 accum_out=lnC[:, 1:2]).then_inc(s_ep2, 1)
                    s.wait_ge(s_ep3, EP3 * glob + 5)
                    s.activation(lnC[:, 3:4], lnC[:, 2:3], AF.Sqrt).then_inc(s_ep2, 1)

            @block.vector
            def _(v, half=half, b_ctr=b_ctr, b_lg=b_lg, b_y2=b_y2, b_ep=b_ep,
                  b_y1=b_y1):
                VS.inc(v.memset(gsb[:], 0.5))
                VS.bar(v)
                for tt in range(NTH):
                    glob = half * NTH + tt
                    if half == 1 and tt == 0:
                        v.wait_ge(s_y1p, NE * Y1G)
                    for dc in range(NDC):
                        n = b_ctr + CTR * tt + dc
                        v.wait_ge(s_ctr, n + 1)
                        v.tensor_copy(x1T[:, dc, tt * 128:(tt + 1) * 128],
                                      ptrC[:, n % 2, 0:128]).then_inc(s_ctrd, 1)
                    v.wait_ge(s_lgp, glob + 1)
                    VS.inc(v.reduce_max(smC[:, 0:1], py1[:, tt % 2, 0:32], axis=AX.X))
                    VS.bar(v)
                    v.tensor_scalar_mul(smC[:, 0:1], smC[:, 0:1], -1.0
                                        ).then_inc(s_sm1, 1)
                    v.wait_ge(s_sm2, glob + 1)
                    VS.inc(v.tensor_copy(wrkC[:], expv[:]))
                    VS.bar(v)
                    for r in range(5):
                        VS.inc(v.reduce_max(smC[:, 1:2], wrkC[:], axis=AX.X))
                        VS.bar(v)
                        VS.inc(v.tensor_scalar(mskC[:], wrkC[:], smC[:, 1:2], None,
                                               op0=ALU.is_equal))
                        VS.bar(v)
                        VS.inc(v.scalar_tensor_tensor(wrkC[:], mskC[:], -BIG, wrkC[:],
                                                      op0=ALU.mult, op1=ALU.add))
                        VS.bar(v)
                    VS.inc(v.reduce_max(smC[:, 1:2], wrkC[:], axis=AX.X))
                    VS.bar(v)
                    VS.inc(v.tensor_scalar(mskC[:], expv[:], smC[:, 1:2], None,
                                           op0=ALU.is_ge))
                    VS.bar(v)
                    VS.inc(v.tensor_tensor(mskC[:], mskC[:], expv[:], op=ALU.mult))
                    VS.bar(v)
                    VS.inc(v.reduce_sum(smC[:, 2:3], mskC[:], axis=AX.X))
                    VS.bar(v)
                    VS.inc(v.reciprocal(smC[:, 2:3], smC[:, 2:3]))
                    VS.bar(v)
                    VS.inc(v.tensor_scalar(gsb[:, tt, 0:32], mskC[:], smC[:, 2:3],
                                           None, op0=ALU.mult))
                    VS.bar(v)
                    if glob >= 1:
                        v.wait_ge(s_ctr, CTR * glob)
                    v.tensor_copy(gbf[:], gsb[:, tt, :]).then_inc(s_lgd, 1)
                    if half == 1 and tt == 0:
                        v.wait_ge(s_y2p, EXP_Y2)
                    n = b_ctr + CTR * tt + 16
                    v.wait_ge(s_ctr, n + 1)
                    v.tensor_copy(gT[:, tt * 128:(tt + 1) * 128],
                                  ptrC[0:NE, n % 2, 0:128]).then_inc(s_ctrd, 1)
                for e in range(NE):
                    for tc in range(NTH):
                        for ds in range(4):
                            m = b_y2 + e * Y2G + tc * 4 + ds
                            v.wait_ge(s_y2p, m + 1)
                            if e == 0:
                                v.tensor_scalar(accC[:, tc, ds * 512:(ds + 1) * 512],
                                                py2[:, m % 4], gsb[:, tc, 0:1], None,
                                                op0=ALU.mult).then_inc(s_y2d, 1)
                            else:
                                ge = gsb[:, tc, e:e + 1]
                                v.scalar_tensor_tensor(
                                    accC[:, tc, ds * 512:(ds + 1) * 512],
                                    py2[:, m % 4], ge,
                                    accC[:, tc, ds * 512:(ds + 1) * 512],
                                    op0=ALU.mult, op1=ALU.add).then_inc(s_y2d, 1)
                for tc in range(NTH):
                    for ds in range(4):
                        m = b_y2 + NE * Y2G + tc * 4 + ds
                        v.wait_ge(s_y2p, m + 1)
                        v.tensor_tensor(accC[:, tc, ds * 512:(ds + 1) * 512],
                                        accC[:, tc, ds * 512:(ds + 1) * 512],
                                        py2[:, m % 4], op=ALU.add).then_inc(s_y2d, 1)
                v.wait_ge(s_y2d, b_y2 + EXP_Y2)
                for tt in range(NTH):
                    glob = half * NTH + tt
                    v.wait_ge(s_ep, 16 * (glob + 1))
                    if glob >= 1:
                        v.wait_ge(s_ep2, 2 * glob)
                        pg_ = glob - 1
                        v.wait_ge(s_osa if pg_ % 2 == 0 else s_osb,
                                  16 * (pg_ // 2 + 1))
                    v.tensor_tensor(resC[:], x1f[:], accC[:, tt], op=ALU.add
                                    ).then_inc(s_ep3, 1)
                    v.wait_ge(s_ep3, EP3 * glob + 1)
                    v.reduce_sum(lnC[:, 0:1], resC[:], axis=AX.X).then_inc(s_ep3, 1)
                    v.wait_ge(s_ep3, EP3 * glob + 2)
                    v.tensor_scalar_mul(lnC[:, 0:1], lnC[:, 0:1], -1.0 / D
                                        ).then_inc(s_ep3, 1)
                    v.wait_ge(s_ep3, EP3 * glob + 3)
                    v.tensor_scalar(resC[:], resC[:], lnC[:, 0:1], None,
                                    op0=ALU.add).then_inc(s_ep3, 1)
                    v.wait_ge(s_ep2, 2 * glob + 1)
                    v.tensor_scalar(lnC[:, 2:3], lnC[:, 1:2], 1.0 / D, 1e-5,
                                    op0=ALU.mult, op1=ALU.add).then_inc(s_ep3, 1)
                    v.wait_ge(s_ep3, EP3 * glob + 5)
                    v.wait_ge(s_ep2, 2 * glob + 2)
                    v.reciprocal(lnC[:, 3:4], lnC[:, 3:4]).then_inc(s_ep3, 1)
                    v.wait_ge(s_ep3, EP3 * glob + 6)
                    v.tensor_scalar(x1f[:], resC[:], lnC[:, 3:4], None,
                                    op0=ALU.mult).then_inc(s_ep3, 1)

            @block.sync
            def _(sy, half=half, t0=t0):
                for tt in range(NTH):
                    glob = half * NTH + tt
                    if glob >= 1:
                        pg_ = glob - 1
                        sy.wait_ge(s_osa if pg_ % 2 == 0 else s_osb,
                                   16 * (pg_ // 2 + 1))
                    sy.dma_start(out=x1f[:], in_=x1_sp[t0 + tt]).then_inc(s_ep, 16)
                    sy.wait_ge(s_ep3, EP3 * (glob + 1))
                    sy.dma_start(out=y_out[(t0 + tt) * 128:(t0 + tt + 1) * 128, :],
                                 in_=x1f[:]).then_inc(
                                     s_osa if glob % 2 == 0 else s_osb, 16)

        sC.close()
        _tap = os.environ.get("KDBG", "")
        if _tap == "mkb":
            s_dbg = sem("s_dbg")

            @block.gpsimd
            def _(g):
                g.wait_ge(s_osa, 16 * (NT // 2))
                g.wait_ge(s_osb, 16 * (NT // 2))
                for i in range(NT):
                    g.dma_start(out=y_out[i * 128:(i + 1) * 128, :],
                                in_=mkb_sp[i][:, 0:2048]).then_inc(s_dbg, 16)
                g.wait_ge(s_dbg, 16 * NT)
        elif _tap:
            _src = {"x1": x1_sp, "mh": mh_sp, "mv": mv_sp}[_tap]
            s_dbg = sem("s_dbg")

            @block.sync
            def _(sy):
                sy.wait_ge(s_osa, 16 * (NT // 2))
                sy.wait_ge(s_osb, 16 * (NT // 2))
                for i in range(NT):
                    sy.dma_start(out=y_out[i * 128:(i + 1) * 128, :], in_=_src[i]
                                 ).then_inc(s_dbg, 16)

            @block.gpsimd
            def _(g):
                g.wait_ge(s_dbg, 16 * NT)
        else:
            @block.gpsimd
            def _(g):
                g.wait_ge(s_osa, 16 * (NT // 2))
                g.wait_ge(s_osb, 16 * (NT // 2))

    return nc


def kernel(**inputs):
    inp = {k: np.asarray(v) for k, v in inputs.items()}
    x = np.ascontiguousarray(inp["L0_attn_hidden_states"].reshape(TOK * NCORES, D))
    w1 = np.concatenate([inp["L0_ffn_rw1"], inp["L0_ffn_sw1"]], axis=0)
    w2 = np.concatenate([inp["L0_ffn_rw2"], inp["L0_ffn_sw2"]], axis=0)
    b1 = np.concatenate([inp["L0_ffn_rb1"], inp["L0_ffn_sb1"]], axis=0)
    b2 = np.concatenate([inp["L0_ffn_rb2"], inp["L0_ffn_sb2"]], axis=0)
    w1h = np.ascontiguousarray(
        w1.reshape(NE, NDC, 128, NHC, 128).transpose(0, 3, 2, 1, 4)
        .reshape(NE, NHC, 128, NDC * 128))
    common = {
        "wkT": np.ascontiguousarray(inp["L0_attn_Wk"].T),
        "wvT": np.ascontiguousarray(inp["L0_attn_Wv"].T),
        "wqT": np.ascontiguousarray(inp["L0_attn_Wq"].T),
        "wgT": np.ascontiguousarray(inp["L0_attn_Wg"].T),
        "wmbT": np.ascontiguousarray(inp["L0_attn_Wmb"].T),
        "bk": inp["L0_attn_bk"], "bv": inp["L0_attn_bv"],
        "bq": inp["L0_attn_bq"], "bg": inp["L0_attn_bg"],
        "bmb": inp["L0_attn_bmb"],
        "mem": inp["L0_attn_memory"], "nrm": inp["L0_attn_norm"],
        "w1h": w1h, "w2h": w2, "b1h": b1, "b2h": b2,
        "rwr": inp["L0_ffn_router_w"], "rbr": inp["L0_ffn_router_b"],
    }
    in_maps = [dict(common, x=np.ascontiguousarray(x[c * TOK:(c + 1) * TOK]))
               for c in range(NCORES)]
    nc = build_nc()
    res = run_bass_kernel_spmd(nc, in_maps, core_ids=list(range(NCORES)))
    out = np.concatenate([res.results[c]["y"] for c in range(NCORES)], axis=0)
    return out.reshape(4, 4096, D)


# version 04 (final)
# speedup vs baseline: 1.0791x; 1.0791x over previous
import os
import numpy as np
from contextlib import ExitStack

import concourse.bass as bass
import concourse.mybir as mybir
from concourse.bass_utils import run_bass_kernel_spmd

F32 = mybir.dt.float32
BF16 = mybir.dt.bfloat16
AF = mybir.ActivationFunctionType
ALU = mybir.AluOpType
AX = mybir.AxisListType

NCORES = 8
TOK = 2048
NT = 16
D = 2048
KV = 512
NH = 16
NKV = 4
F = 768
NFC = 6
EPS = 1e-8
H_FFN = 1408
NHC = 11
NE = 34
NDC = 16
HALF = 1024
NTH = 8
BIG = 1e30


class _SelfSem:

    def __init__(self, semh):
        self.s = semh
        self.n = 0

    def inc(self, instr):
        instr.then_inc(self.s, 1)
        self.n += 1
        return instr

    def bar(self, eng):
        eng.wait_ge(self.s, self.n)


def _make_identity(g, identity, semh, base):
    sq = identity.shape[0]
    g.memset(identity, 0.0).then_inc(semh, 1)
    g.wait_ge(semh, base + 1)
    g.affine_select(out=identity, in_=identity, compare_op=ALU.not_equal,
                    fill=1.0, base=0, pattern=[[-1, sq]], channel_multiplier=1,
                    ).then_inc(semh, 1)
    return base + 2


def build_nc():
    nc = bass.Bass("TRN2", target_bir_lowering=False,
                   detect_race_conditions=False)

    x_in = nc.dram_tensor("x", [TOK, D], F32, kind="ExternalInput").ap()
    wkT = nc.dram_tensor("wkT", [D, KV], F32, kind="ExternalInput").ap()
    wvT = nc.dram_tensor("wvT", [D, KV], F32, kind="ExternalInput").ap()
    wqT = nc.dram_tensor("wqT", [D, D], F32, kind="ExternalInput").ap()
    wgT = nc.dram_tensor("wgT", [D, D], F32, kind="ExternalInput").ap()
    wmbT = nc.dram_tensor("wmbT", [D, D], F32, kind="ExternalInput").ap()
    bk = nc.dram_tensor("bk", [KV], F32, kind="ExternalInput").ap()
    bv = nc.dram_tensor("bv", [KV], F32, kind="ExternalInput").ap()
    bq = nc.dram_tensor("bq", [D], F32, kind="ExternalInput").ap()
    bg = nc.dram_tensor("bg", [D], F32, kind="ExternalInput").ap()
    bmb = nc.dram_tensor("bmb", [D], F32, kind="ExternalInput").ap()
    mem_in = nc.dram_tensor("mem", [NH, F, 128], F32, kind="ExternalInput").ap()
    nrm_in = nc.dram_tensor("nrm", [NH, F], F32, kind="ExternalInput").ap()
    w1h = nc.dram_tensor("w1h", [NE, NHC, 128, NDC * 128], F32,
                         kind="ExternalInput").ap()
    w2h = nc.dram_tensor("w2h", [NE, H_FFN, D], F32, kind="ExternalInput").ap()
    b1h = nc.dram_tensor("b1h", [NE, H_FFN], F32, kind="ExternalInput").ap()
    b2h = nc.dram_tensor("b2h", [NE, D], F32, kind="ExternalInput").ap()
    rwr = nc.dram_tensor("rwr", [D, 32], F32, kind="ExternalInput").ap()
    rbr = nc.dram_tensor("rbr", [32], F32, kind="ExternalInput").ap()
    y_out = nc.dram_tensor("y", [TOK, D], F32, kind="ExternalOutput").ap()

    xT_sp = nc.dram_tensor("xT_sp", [NT, 128, D], BF16).ap()
    mkb_sp = nc.dram_tensor("mkb_sp", [NT, 128, NKV * F], BF16).ap()
    mv_sp = nc.dram_tensor("mv_sp", [NT, 128, D], F32).ap()
    mh_sp = nc.dram_tensor("mh_sp", [NT, 128, D], F32).ap()
    x1_sp = nc.dram_tensor("x1_sp", [NT, 128, D], F32).ap()
    cc_in = nc.dram_tensor("cc_in", [128, NKV * NFC * 516], F32).ap()
    cc_out = nc.dram_tensor("cc_out", [128, NKV * NFC * 516], F32).ap()

    ctx = ExitStack()
    with ctx:
        def sem(name):
            return ctx.enter_context(nc.semaphore(name))

        s_ph = sem("s_ph")
        VS = _SelfSem(sem("s_vrt"))
        s_p0 = sem("s_p0")
        s_ini = sem("s_ini")
        s_p0t = sem("s_p0t")
        s_p0d = sem("s_p0d")
        s_xb0 = sem("s_xb0")
        s_xb1 = sem("s_xb1")
        s_tr = sem("s_tr")
        s_trd = sem("s_trd")
        s_kvp = sem("s_kvp")
        s_kvd = sem("s_kvd")
        s_f = sem("s_f")
        s_ss = sem("s_ss")
        s_sq = sem("s_sq")
        s_mkb = sem("s_mkb")
        s_mksq = sem("s_mksq")
        s_nump = sem("s_nump")
        s_numd = sem("s_numd")
        s_tdone = sem("s_tdone")
        s_xts = sem("s_xts")
        s_mks = sem("s_mks")
        s_mvs = sem("s_mvs")
        s_wmb = sem("s_wmb")
        s_mk2a = sem("s_mk2a")
        s_mk2b = sem("s_mk2b")
        s_xT2l = sem("s_xT2l")
        s_mv2l = sem("s_mv2l")
        s_mbp = sem("s_mbp")
        s_mbd = sem("s_mbd")
        s_wmv = sem("s_wmv")
        s_dmp = sem("s_dmp")
        s_dmd = sem("s_dmd")
        s_ar = sem("s_ar")
        s_arg = sem("s_arg")
        s_cc = sem("s_cc")
        s_nm = sem("s_nm")
        s_wql = sem("s_wql")
        s_xT3l = sem("s_xT3l")
        s_qp = sem("s_qp")
        s_qd = sem("s_qd")
        s_bf = sem("s_bf")
        s_bss = sem("s_bss")
        s_bsq = sem("s_bsq")
        s_bmk = sem("s_bmk")
        s_btr = sem("s_btr")
        s_btrd = sem("s_btrd")
        s_nap = sem("s_nap")
        s_nad = sem("s_nad")
        s_mhs = sem("s_mhs")
        s_wgl = sem("s_wgl")
        s_xT4l = sem("s_xT4l")
        s_mh2l = sem("s_mh2l")
        s_xf2a = sem("s_xf2a")
        s_xf2b = sem("s_xf2b")
        s_gp = sem("s_gp")
        s_gd = sem("s_gd")
        s_r1 = sem("s_r1")
        s_sq2 = sem("s_sq2")
        s_sqr = sem("s_sqr")
        s_x1s = sem("s_x1s")
        s_cl = sem("s_cl")
        s_ctr = sem("s_ctr")
        s_ctrd = sem("s_ctrd")
        s_lgp = sem("s_lgp")
        s_sm1 = sem("s_sm1")
        s_sm2 = sem("s_sm2")
        s_lgd = sem("s_lgd")
        s_gtp = sem("s_gtp")
        s_gtd = sem("s_gtd")
        s_w1s = [sem("s_w1s0"), sem("s_w1s1"), sem("s_w1s2"), sem("s_w1s3")]
        s_w2 = sem("s_w2")
        s_y1p = sem("s_y1p")
        s_y1d = sem("s_y1d")
        s_y2p = sem("s_y2p")
        s_y2d = sem("s_y2d")
        s_ep = sem("s_ep")
        s_ep2 = sem("s_ep2")
        s_ep3 = sem("s_ep3")
        s_osa = sem("s_osa")
        s_osb = sem("s_osb")

        ident = ctx.enter_context(nc.sbuf_tensor("ident", [128, 128], BF16))
        ones = ctx.enter_context(nc.sbuf_tensor("ones", [1, 128], BF16))
        rbrrow = ctx.enter_context(nc.sbuf_tensor("rbrrow", [1, 32], BF16))
        b1col = ctx.enter_context(nc.sbuf_tensor("b1col", [128, NHC, NE], F32))
        b2raw = ctx.enter_context(nc.sbuf_tensor("b2raw", [NE, D], BF16))
        rwsb = ctx.enter_context(nc.sbuf_tensor("rwsb", [128, NDC, 32], BF16))
        epsb = ctx.enter_context(nc.sbuf_tensor("epsb", [128, 1], F32))

        block = ctx.enter_context(nc.Block())

        PH = {p: 5 * i for i, p in enumerate(
            ["P0", "A1", "A2", "AR", "B1", "B2", "C"])}

        def _barrier(next_val, waits=None):
            waits = waits or {}
            for name, emitter in (("gpsimd", block.gpsimd), ("tensor", block.tensor),
                                  ("vector", block.vector), ("scalar", block.scalar),
                                  ("sync", block.sync)):
                def _body(e, nv=next_val, ws=waits.get(name, [])):
                    for semh, cnt in ws:
                        e.wait_ge(semh, cnt)
                    e.nop().then_inc(s_ph, 1)
                    e.wait_ge(s_ph, nv)
                emitter(_body)

        attn_ctx = ExitStack()

        def asb(*a):
            return attn_ctx.enter_context(nc.sbuf_tensor(*a))

        identF = asb("identF", [128, 128], F32)
        normT = asb("normT", [128, NFC, NH], F32)
        coef = asb("coef", [128, NT, NH], F32)
        bkrow = asb("bkrow", [1, KV], BF16)
        bvrow = asb("bvrow", [1, KV], BF16)
        bqrow = asb("bqrow", [1, D], BF16)
        bgrow = asb("bgrow", [1, D], BF16)
        bmbrow = asb("bmbrow", [1, D], BF16)
        nrmsb = asb("nrmsb", [NH, F], F32)
        b1bf = asb("b1bf", [NE, H_FFN], BF16)

        sA = ExitStack()

        def a1sb(*a):
            return sA.enter_context(nc.sbuf_tensor(*a))

        def a1ps(*a):
            return sA.enter_context(nc.psum_tensor(*a))

        wk_sb = a1sb("wk_sb", [128, NDC, KV], BF16)
        wv_sb = a1sb("wv_sb", [128, NDC, KV], BF16)
        mext = a1sb("mext", [128, NH, NFC, 129], BF16)
        xb = a1sb("xb", [128, 2, D], BF16)
        xT = a1sb("xT", [128, NDC, 128], BF16)
        kf = a1sb("kf", [128, KV], F32)
        vf = a1sb("vf", [128, KV], F32)
        rr = a1sb("rr", [128, 256], F32)
        fbuf = a1sb("fbuf", [128, F], F32)
        fsq = a1sb("fsq", [128, F], F32)
        mkb = a1sb("mkb", [128, NKV, F], BF16)
        mkT = a1sb("mkT", [128, NKV, NFC, 128], BF16)
        mvb = a1sb("mvb", [128, D], F32)
        ssb = a1sb("ssb", [128, NKV], F32)
        sqb = a1sb("sqb", [128, NKV], F32)
        rsb = a1sb("rsb", [128, NKV], F32)
        mksq = a1sb("mksq", [128, NKV], F32)
        rmksq = a1sb("rmksq", [128, NKV], F32)
        dnE = a1sb("dnE", [128, NH], F32)
        rdnn = a1sb("rdnn", [128, NH], F32)
        ctmp = a1sb("ctmp", [128, NH], F32)
        ptr = a1ps("ptr", [128, 2, 1024], BF16)
        ptrF = a1ps("ptrF", [128, 512], F32)
        pkv = a1ps("pkv", [128, 2, 512], F32)
        pnum = a1ps("pnum", [128, 3, 512], F32)

        P0_G_INIT = 6
        P0_DMAS = 6 + 1 + 1 + 1 + 1 + NH + 1 + 1
        P0_G_TOTAL = 16 * P0_DMAS

        @block.gpsimd
        def _(g):
            base = _make_identity(g, ident[:], s_ini, 0)
            base = _make_identity(g, identF[:], s_ini, base)
            g.memset(ones[:], 1.0).then_inc(s_ini, 1)
            g.memset(epsb[:], 1e-24).then_inc(s_ini, 1)
            g.dma_start(out=bkrow[:], in_=bk[None, :]).then_inc(s_p0, 16)
            g.dma_start(out=bvrow[:], in_=bv[None, :]).then_inc(s_p0, 16)
            g.dma_start(out=bqrow[:], in_=bq[None, :]).then_inc(s_p0, 16)
            g.dma_start(out=bgrow[:], in_=bg[None, :]).then_inc(s_p0, 16)
            g.dma_start(out=bmbrow[:], in_=bmb[None, :]).then_inc(s_p0, 16)
            g.dma_start(out=rbrrow[:], in_=rbr[None, :]).then_inc(s_p0, 16)
            g.dma_start(out=nrmsb[:], in_=nrm_in[:, :]).then_inc(s_p0, 16)
            g.dma_start(out=b1bf[:], in_=b1h[:, :]).then_inc(s_p0, 16)
            g.dma_start(out=b2raw[:], in_=b2h[:, :]).then_inc(s_p0, 16)
            g.dma_start(out=rwsb[:], in_=rwr.rearrange("(dc p) e -> p dc e", p=128)
                        ).then_inc(s_p0, 16)
            for h in range(NH):
                g.dma_start(out=mext[:, h, :, 0:128],
                            in_=mem_in[h].rearrange("(fc p) d -> p fc d", p=128)
                            ).then_inc(s_p0, 16)
            g.dma_start(out=wk_sb[:], in_=wkT.rearrange("(dc p) o -> p dc o", p=128)
                        ).then_inc(s_p0, 16)
            g.dma_start(out=wv_sb[:], in_=wvT.rearrange("(dc p) o -> p dc o", p=128)
                        ).then_inc(s_p0, 16)

        @block.tensor
        def _(t):
            t.wait_ge(s_ini, P0_G_INIT)
            t.wait_ge(s_p0, P0_G_TOTAL)
            for fc in range(NFC):
                if fc >= 1:
                    t.wait_ge(s_p0d, fc)
                t.transpose(ptrF[:, 0:NH], nrmsb[:, fc * 128:(fc + 1) * 128],
                            identF[0:NH, 0:NH]).then_inc(s_p0t, 1)
            for hc in range(NHC):
                if hc >= 2:
                    t.wait_ge(s_p0d, NFC + hc - 1)
                t.transpose(ptr[:, hc % 2, 0:NE],
                            b1bf[:, hc * 128:(hc + 1) * 128], ident[0:NE, 0:NE]
                            ).then_inc(s_p0t, 1)

        @block.vector
        def _(v):
            for fc in range(NFC):
                v.wait_ge(s_p0t, fc + 1)
                v.tensor_copy(normT[:, fc, :], ptrF[:, 0:NH]).then_inc(s_p0d, 1)
            for hc in range(NHC):
                v.wait_ge(s_p0t, NFC + hc + 1)
                v.tensor_copy(b1col[:, hc, :], ptr[:, hc % 2, 0:NE]).then_inc(s_p0d, 1)
            v.wait_ge(s_p0, P0_G_TOTAL)
            v.tensor_copy(mext[:, :, :, 128],
                          normT.rearrange("p fc h -> p h fc")).then_inc(s_p0d, 1)

        _barrier(PH["A1"], {
            "gpsimd": [(s_p0, 16 * P0_DMAS), (s_ini, P0_G_INIT)],
            "tensor": [(s_p0t, NFC + NHC)],
            "vector": [(s_p0d, NFC + NHC + 1)],
        })

        @block.gpsimd
        def _(g):
            for i in range(NT):
                if i >= 2:
                    g.wait_ge(s_tr, 40 * (i - 1))
                g.dma_start(out=xb[:, i % 2], in_=x_in[i * 128:(i + 1) * 128, :]
                            ).then_inc(s_xb0 if i % 2 == 0 else s_xb1, 16)

        @block.tensor
        def _(t):
            for i in range(NT):
                t.wait_ge(s_xb0 if i % 2 == 0 else s_xb1, 16 * (i // 2 + 1))
                t.wait_ge(s_xts, 16 * i)
                for j in range(16):
                    n = 40 * i + j
                    if n >= 2:
                        t.wait_ge(s_trd, n - 1)
                    t.transpose(ptr[:, n % 2, 0:128],
                                xb[:, i % 2, j * 128:(j + 1) * 128],
                                ident[:]).then_inc(s_tr, 1)
                t.wait_ge(s_trd, 40 * i + 16)
                t.wait_ge(s_kvd, 2 * i)
                for dc in range(NDC):
                    t.matmul(pkv[:, 0], xT[:, dc, :], wk_sb[:, dc, :],
                             start=(dc == 0), stop=False)
                t.matmul(pkv[:, 0], ones[:, 0:128], bkrow[:],
                         start=False, stop=True).then_inc(s_kvp, 1)
                for dc in range(NDC):
                    t.matmul(pkv[:, 1], xT[:, dc, :], wv_sb[:, dc, :],
                             start=(dc == 0), stop=False)
                t.matmul(pkv[:, 1], ones[:, 0:128], bvrow[:],
                         start=False, stop=True).then_inc(s_kvp, 1)
                for kv in range(NKV):
                    t.wait_ge(s_mkb, 4 * i + kv + 1)
                    for fc in range(NFC):
                        n = 40 * i + 16 + kv * NFC + fc
                        t.wait_ge(s_trd, n - 1)
                        t.transpose(ptr[:, n % 2, 0:128],
                                    mkb[:, kv, fc * 128:(fc + 1) * 128],
                                    ident[:]).then_inc(s_tr, 1)
                for h in range(NH):
                    kv = h // 4
                    t.wait_ge(s_trd, 40 * i + 16 + NFC * (kv + 1))
                    if 16 * i + h >= 3:
                        t.wait_ge(s_numd, 16 * i + h - 2)
                    for fc in range(NFC):
                        mm = t.matmul(pnum[:, h % 3, 0:129], mkT[:, kv, fc, :],
                                      mext[:, h, fc, :],
                                      start=(fc == 0), stop=(fc == NFC - 1))
                    mm.then_inc(s_nump, 1)

        @block.scalar
        def _(s):
            for i in range(NT):
                s.wait_ge(s_tdone, i)
                s.wait_ge(s_kvp, 2 * i + 1)
                s.copy(kf[:], pkv[:, 0]).then_inc(s_kvd, 1)
                s.wait_ge(s_kvp, 2 * i + 2)
                s.copy(vf[:], pkv[:, 1]).then_inc(s_kvd, 1)
                for kv in range(NKV):
                    s.wait_ge(s_f, 6 * (4 * i + kv + 1))
                    s.activation(fsq[:], fbuf[:], AF.Square,
                                 accum_out=ssb[:, kv:kv + 1]).then_inc(s_ss, 1)
                    s.wait_ge(s_ss, 4 * i + kv + 1)
                    s.activation(sqb[:, kv:kv + 1], ssb[:, kv:kv + 1], AF.Sqrt,
                                 bias=epsb[:]).then_inc(s_sq, 1)
                    s.wait_ge(s_mkb, 4 * i + kv + 1)
                    s.activation(fsq[:], mkb[:, kv, :], AF.Square,
                                 accum_out=mksq[:, kv:kv + 1]).then_inc(s_mksq, 1)

        @block.vector
        def _(v):
            for i in range(NT):
                v.wait_ge(s_kvp, 2 * i)
                v.wait_ge(s_xts, 16 * i)
                for j in range(16):
                    n = 40 * i + j
                    v.wait_ge(s_tr, n + 1)
                    v.tensor_copy(xT[:, j, :], ptr[:, n % 2, 0:128]).then_inc(s_trd, 1)
                v.wait_ge(s_tr, 40 * i)
                v.wait_ge(s_mks, 16 * i)
                v.wait_ge(s_mksq, 4 * i)
                v.wait_ge(s_nump, 16 * i)
                for kv in range(NKV):
                    v.wait_ge(s_kvd, 2 * i + 1)
                    if 4 * i + kv >= 1:
                        v.wait_ge(s_ss, 4 * i + kv)
                    kcol = kf[:, kv * 128:(kv + 1) * 128]
                    VS.inc(v.tensor_scalar_max(rr[:, 0:128], kcol, 0.0))
                    VS.inc(v.tensor_scalar(rr[:, 128:256], kcol, -1.0, 0.0,
                                           op0=ALU.mult, op1=ALU.max))
                    VS.bar(v)
                    for jj in range(3):
                        sft = jj + 1
                        base = jj * 256
                        v.tensor_tensor(fbuf[:, base + sft:base + 256],
                                        rr[:, sft:256], rr[:, 0:256 - sft],
                                        op=ALU.mult).then_inc(s_f, 1)
                        v.tensor_tensor(fbuf[:, base:base + sft],
                                        rr[:, 0:sft], rr[:, 256 - sft:256],
                                        op=ALU.mult).then_inc(s_f, 1)
                    v.wait_ge(s_f, 6 * (4 * i + kv + 1))
                    v.wait_ge(s_sq, 4 * i + kv + 1)
                    VS.inc(v.reciprocal(rsb[:, kv:kv + 1], sqb[:, kv:kv + 1]))
                    VS.bar(v)
                    v.tensor_scalar(mkb[:, kv, :], fbuf[:], rsb[:, kv:kv + 1], None,
                                    op0=ALU.mult).then_inc(s_mkb, 1)
                    for fc in range(NFC):
                        n = 40 * i + 16 + kv * NFC + fc
                        v.wait_ge(s_tr, n + 1)
                        v.tensor_copy(mkT[:, kv, fc, :], ptr[:, n % 2, 0:128]
                                      ).then_inc(s_trd, 1)
                v.wait_ge(s_mvs, 16 * i)
                for h in range(NH):
                    kv = h // 4
                    v.wait_ge(s_nump, 16 * i + h + 1)
                    VS.inc(v.tensor_scalar_add(dnE[:, h:h + 1],
                                               pnum[:, h % 3, 128:129], EPS))
                    VS.bar(v)
                    VS.inc(v.reciprocal(rdnn[:, h:h + 1], dnE[:, h:h + 1]))
                    VS.bar(v)
                    VS.inc(v.tensor_scalar_mul(rdnn[:, h:h + 1], rdnn[:, h:h + 1],
                                               -1.0))
                    VS.bar(v)
                    v.scalar_tensor_tensor(mvb[:, h * 128:(h + 1) * 128],
                                           pnum[:, h % 3, 0:128], rdnn[:, h:h + 1],
                                           vf[:, kv * 128:(kv + 1) * 128],
                                           op0=ALU.mult, op1=ALU.add
                                           ).then_inc(s_numd, 1)
                v.wait_ge(s_mksq, 4 * (i + 1))
                VS.inc(v.reciprocal(rmksq[:], mksq[:]))
                VS.bar(v)
                for kv in range(NKV):
                    VS.inc(v.tensor_scalar(ctmp[:, 4 * kv:4 * kv + 4],
                                           dnE[:, 4 * kv:4 * kv + 4],
                                           rmksq[:, kv:kv + 1],
                                           None, op0=ALU.mult))
                VS.bar(v)
                VS.inc(v.tensor_scalar(ctmp[:], ctmp[:], -1.0, 1.0,
                                       op0=ALU.mult, op1=ALU.add))
                VS.bar(v)
                v.tensor_scalar(coef[:, i, :], ctmp[:], 0.0, 1.0,
                                op0=ALU.max, op1=ALU.min).then_inc(s_tdone, 1)

        @block.sync
        def _(sy):
            for i in range(NT):
                sy.wait_ge(s_trd, 40 * i + 16)
                sy.dma_start(out=xT_sp[i].rearrange("p (dc t) -> p dc t", dc=NDC),
                             in_=xT[:]).then_inc(s_xts, 16)
                sy.wait_ge(s_mkb, 4 * (i + 1))
                sy.dma_start(out=mkb_sp[i].rearrange("p (kv f) -> p kv f", kv=NKV),
                             in_=mkb[:]).then_inc(s_mks, 16)
                sy.wait_ge(s_numd, 16 * (i + 1))
                sy.wait_ge(s_tdone, i + 1)
                sy.dma_start(out=mv_sp[i], in_=mvb[:]).then_inc(s_mvs, 16)

        sA.close()
        _barrier(PH["A2"], {
            "gpsimd": [(s_xb0, 16 * 8), (s_xb1, 16 * 8)],
            "tensor": [(s_nump, 16 * NT), (s_kvp, 2 * NT), (s_tr, 40 * NT)],
            "scalar": [(s_mksq, 4 * NT), (s_kvd, 2 * NT), (s_sq, 4 * NT)],
            "vector": [(s_tdone, NT), (s_numd, 16 * NT), (s_trd, 40 * NT),
                       (s_mkb, 4 * NT), (s_f, 24 * NT), (VS.s, VS.n)],
            "sync": [(s_xts, 16 * NT), (s_mks, 16 * NT), (s_mvs, 16 * NT)],
        })

        sA2 = ExitStack()
        dmem = sA2.enter_context(nc.sbuf_tensor("dmem", [128, NKV, NFC, 516], F32))
        sA2a = ExitStack()
        wmb_sb = sA2a.enter_context(nc.sbuf_tensor("wmb_sb", [128, NDC, D], BF16))
        mkb2 = sA2a.enter_context(nc.sbuf_tensor("mkb2", [128, 2, NKV, F], BF16))
        xT2 = sA2a.enter_context(nc.sbuf_tensor("xT2", [128, NDC, 128], BF16))
        mv2 = sA2a.enter_context(nc.sbuf_tensor("mv2", [128, D], F32))
        mbf = sA2a.enter_context(nc.sbuf_tensor("mbf", [128, D], F32))
        wmv2 = sA2a.enter_context(nc.sbuf_tensor("wmv2", [128, 2, NKV, 516], BF16))
        pmb = sA2a.enter_context(nc.psum_tensor("pmb", [128, 2, 512], F32))
        pdm = sA2a.enter_context(nc.psum_tensor("pdm", [128, 2, 512], F32))
        pdn = sA2a.enter_context(nc.psum_tensor("pdn", [128, 2, 512], F32))

        @block.gpsimd
        def _(g):
            g.dma_start(out=wmb_sb[:], in_=wmbT.rearrange("(dc p) o -> p dc o", p=128)
                        ).then_inc(s_wmb, 16)
            for i in range(NT):
                gi = i // 2
                if i >= 2:
                    g.wait_ge(s_dmp, 24 * gi)
                if i >= 1:
                    g.wait_ge(s_mbp, 4 * i)
                    g.wait_ge(s_wmv, 8 * i)
                g.dma_start(out=mkb2[:, i % 2],
                            in_=mkb_sp[i].rearrange("p (kv f) -> p kv f", kv=NKV)
                            ).then_inc(s_mk2a if i % 2 == 0 else s_mk2b, 16)
                g.dma_start(out=xT2[:], in_=xT_sp[i].rearrange(
                    "p (dc t) -> p dc t", dc=NDC)).then_inc(s_xT2l, 16)
                g.dma_start(out=mv2[:], in_=mv_sp[i]).then_inc(s_mv2l, 16)

        @block.tensor
        def _(t):
            for i in range(NT):
                gi, ti = i // 2, i % 2
                t.wait_ge(s_wmb, 16)
                t.wait_ge(s_xT2l, 16 * (i + 1))
                for sl in range(4):
                    m = 4 * i + sl
                    if m >= 2:
                        t.wait_ge(s_mbd, m - 1)
                    for dc in range(NDC):
                        t.matmul(pmb[:, m % 2], xT2[:, dc, :],
                                 wmb_sb[:, dc, sl * 512:(sl + 1) * 512],
                                 start=(dc == 0), stop=False)
                    t.matmul(pmb[:, m % 2], ones[:, 0:128],
                             bmbrow[:, sl * 512:(sl + 1) * 512],
                             start=False, stop=True).then_inc(s_mbp, 1)
                if ti == 1:
                    t.wait_ge(s_wmv, 8 * (i + 1))
                    t.wait_ge(s_mk2a, 16 * (gi + 1))
                    t.wait_ge(s_mk2b, 16 * (gi + 1))
                    for kv in range(NKV):
                        for fc in range(NFC):
                            n = 24 * gi + kv * NFC + fc
                            if n >= 2:
                                t.wait_ge(s_dmd, 2 * (n - 1))
                            for tt in range(2):
                                t.matmul(pdm[:, n % 2],
                                         mkb2[:, tt, kv, fc * 128:(fc + 1) * 128],
                                         wmv2[:, tt, kv, 0:512],
                                         start=(tt == 0), stop=(tt == 1))
                                mm = t.matmul(pdn[:, n % 2, 0:4],
                                              mkb2[:, tt, kv, fc * 128:(fc + 1) * 128],
                                              wmv2[:, tt, kv, 512:516],
                                              start=(tt == 0), stop=(tt == 1))
                            mm.then_inc(s_dmp, 1)

        @block.scalar
        def _(s):
            for i in range(NT):
                for sl in range(4):
                    m = 4 * i + sl
                    s.wait_ge(s_mbp, m + 1)
                    if i >= 1:
                        s.wait_ge(s_wmv, 8 * i)
                    s.activation(mbf[:, sl * 512:(sl + 1) * 512], pmb[:, m % 2],
                                 AF.Sigmoid).then_inc(s_mbd, 1)

        @block.vector
        def _(v):
            v.memset(dmem[:], 0.0)
            for i in range(NT):
                gi, ti = i // 2, i % 2
                v.wait_ge(s_mbd, 4 * (i + 1))
                v.wait_ge(s_mv2l, 16 * (i + 1))
                if i >= 2:
                    v.wait_ge(s_dmp, 24 * gi)
                for kv in range(NKV):
                    v.tensor_tensor(wmv2[:, ti, kv, 0:512],
                                    mv2[:, kv * 512:(kv + 1) * 512],
                                    mbf[:, kv * 512:(kv + 1) * 512],
                                    op=ALU.mult).then_inc(s_wmv, 1)
                    v.tensor_copy(wmv2[:, ti, kv, 512:516],
                                  coef[:, i, 4 * kv:4 * kv + 4]).then_inc(s_wmv, 1)
                if ti == 1:
                    for kv in range(NKV):
                        for fc in range(NFC):
                            n = 24 * gi + kv * NFC + fc
                            v.wait_ge(s_dmp, n + 1)
                            v.tensor_tensor(dmem[:, kv, fc, 0:512],
                                            dmem[:, kv, fc, 0:512], pdm[:, n % 2],
                                            op=ALU.add).then_inc(s_dmd, 1)
                            v.tensor_tensor(dmem[:, kv, fc, 512:516],
                                            dmem[:, kv, fc, 512:516],
                                            pdn[:, n % 2, 0:4],
                                            op=ALU.add).then_inc(s_dmd, 1)

        sA2a.close()
        _barrier(PH["AR"], {
            "gpsimd": [(s_wmb, 16), (s_mk2a, 16 * 8), (s_mk2b, 16 * 8),
                       (s_xT2l, 16 * NT), (s_mv2l, 16 * NT)],
            "tensor": [(s_dmp, 24 * 8), (s_mbp, 4 * NT)],
            "scalar": [(s_mbd, 4 * NT)],
            "vector": [(s_dmd, 2 * 24 * 8), (s_wmv, 8 * NT), (VS.s, VS.n)],
        })

        sAR = ExitStack()
        nmem = sAR.enter_context(nc.sbuf_tensor("nmem", [128, NH, NFC, 129], BF16))
        sARm = ExitStack()
        memf = sARm.enter_context(nc.sbuf_tensor("memf", [128, NH, NFC, 128], F32))
        NM_TOT = NH * NFC

        @block.sync
        def _(sy):
            sy.dma_start(out=cc_in[:, :],
                         in_=dmem.rearrange("p kv fc c -> p (kv fc c)")
                         ).then_inc(s_ar, 16)

        @block.gpsimd
        def _(g):
            g.dma_start(out=memf[:],
                        in_=mem_in.rearrange("h (fc p) d -> p h fc d", p=128)
                        ).then_inc(s_arg, 16)
            g.wait_ge(s_ar, 16)
            g.wait_ge(s_arg, 16)
            g.collective_compute(
                "AllReduce", ALU.add,
                replica_groups=[list(range(NCORES))],
                ins=[cc_in[:, :].opt()], outs=[cc_out[:, :].opt()],
            ).then_inc(s_cc, 1)
            g.wait_ge(s_cc, 1)
            g.dma_start(out=dmem.rearrange("p kv fc c -> p (kv fc c)"),
                        in_=cc_out[:, :]).then_inc(s_arg, 16)

        @block.vector
        def _(v):
            v.wait_ge(s_arg, 32)
            v.wait_ge(s_cc, 1)
            for h in range(NH):
                kv, j = h // 4, h % 4
                for fc in range(NFC):
                    v.tensor_tensor(nmem[:, h, fc, 0:128], memf[:, h, fc, :],
                                    dmem[:, kv, fc, j * 128:(j + 1) * 128],
                                    op=ALU.add).then_inc(s_nm, 1)
                    v.tensor_tensor(nmem[:, h, fc, 128:129], normT[:, fc, h:h + 1],
                                    dmem[:, kv, fc, 512 + j:513 + j],
                                    op=ALU.add).then_inc(s_nm, 1)

        sARm.close()
        _barrier(PH["B1"], {
            "gpsimd": [(s_arg, 32), (s_cc, 1)],
            "sync": [(s_ar, 16)],
            "vector": [(s_nm, 2 * NH * NFC), (VS.s, VS.n)],
        })

        sB1 = ExitStack()
        wq_sb = sB1.enter_context(nc.sbuf_tensor("wq_sb", [128, NDC, D], BF16))
        xT3 = sB1.enter_context(nc.sbuf_tensor("xT3", [128, NDC, 128], BF16))
        qf = sB1.enter_context(nc.sbuf_tensor("qf", [128, D], F32))
        rr3 = sB1.enter_context(nc.sbuf_tensor("rr3", [128, 256], F32))
        fb3 = sB1.enter_context(nc.sbuf_tensor("fb3", [128, F], F32))
        fs3 = sB1.enter_context(nc.sbuf_tensor("fs3", [128, F], F32))
        mqb = sB1.enter_context(nc.sbuf_tensor("mqb", [128, F], BF16))
        mqT = sB1.enter_context(nc.sbuf_tensor("mqT", [128, NFC, 128], BF16))
        mhb = sB1.enter_context(nc.sbuf_tensor("mhb", [128, D], F32))
        bssq = sB1.enter_context(nc.sbuf_tensor("bssq", [128, 2], F32))
        bsqr = sB1.enter_context(nc.sbuf_tensor("bsqr", [128, 2], F32))
        brs = sB1.enter_context(nc.sbuf_tensor("brs", [128, 2], F32))
        bdna = sB1.enter_context(nc.sbuf_tensor("bdna", [128, 2], F32))
        ptr3 = sB1.enter_context(nc.psum_tensor("ptr3", [128, 2, 1024], BF16))
        pq = sB1.enter_context(nc.psum_tensor("pq", [128, 2, 512], F32))
        pna = sB1.enter_context(nc.psum_tensor("pna", [128, 3, 512], F32))

        @block.gpsimd
        def _(g):
            g.dma_start(out=wq_sb[:], in_=wqT.rearrange("(dc p) o -> p dc o", p=128)
                        ).then_inc(s_wql, 16)
            for i in range(NT):
                if i >= 1:
                    g.wait_ge(s_qp, 4 * i)
                g.dma_start(out=xT3[:], in_=xT_sp[i].rearrange(
                    "p (dc t) -> p dc t", dc=NDC)).then_inc(s_xT3l, 16)

        @block.tensor
        def _(t):
            for i in range(NT):
                t.wait_ge(s_wql, 16)
                t.wait_ge(s_xT3l, 16 * (i + 1))
                for sl in range(4):
                    m = 4 * i + sl
                    if m >= 2:
                        t.wait_ge(s_qd, m - 1)
                    for dc in range(NDC):
                        t.matmul(pq[:, m % 2], xT3[:, dc, :],
                                 wq_sb[:, dc, sl * 512:(sl + 1) * 512],
                                 start=(dc == 0), stop=False)
                    t.matmul(pq[:, m % 2], ones[:, 0:128],
                             bqrow[:, sl * 512:(sl + 1) * 512],
                             start=False, stop=True).then_inc(s_qp, 1)
                for h in range(NH):
                    hh = NH * i + h
                    t.wait_ge(s_bmk, hh + 1)
                    for fc in range(NFC):
                        n = 6 * hh + fc
                        if n >= 2:
                            t.wait_ge(s_btrd, n - 1)
                        t.transpose(ptr3[:, n % 2, 0:128],
                                    mqb[:, fc * 128:(fc + 1) * 128],
                                    ident[:]).then_inc(s_btr, 1)
                    t.wait_ge(s_btrd, 6 * (hh + 1))
                    if hh >= 3:
                        t.wait_ge(s_nad, hh - 2)
                    for fc in range(NFC):
                        mm = t.matmul(pna[:, hh % 3, 0:129], mqT[:, fc, :],
                                      nmem[:, h, fc, :],
                                      start=(fc == 0), stop=(fc == NFC - 1))
                    mm.then_inc(s_nap, 1)

        @block.scalar
        def _(s):
            for i in range(NT):
                for sl in range(4):
                    m = 4 * i + sl
                    s.wait_ge(s_qp, m + 1)
                    if i >= 1:
                        s.wait_ge(s_bf, 6 * NH * i)
                    s.copy(qf[:, sl * 512:(sl + 1) * 512], pq[:, m % 2]
                           ).then_inc(s_qd, 1)
                for h in range(NH):
                    hh = NH * i + h
                    s.wait_ge(s_bf, 6 * (hh + 1))
                    if hh >= 2:
                        s.wait_ge(s_bmk, hh - 1)
                    s.activation(fs3[:], fb3[:], AF.Square,
                                 accum_out=bssq[:, hh % 2:hh % 2 + 1]).then_inc(s_bss, 1)
                    s.wait_ge(s_bss, hh + 1)
                    if hh >= 2:
                        s.wait_ge(s_bmk, hh - 1)
                    s.activation(bsqr[:, hh % 2:hh % 2 + 1], bssq[:, hh % 2:hh % 2 + 1],
                                 AF.Sqrt, bias=epsb[:]).then_inc(s_bsq, 1)

        @block.vector
        def _(v):
            for i in range(NT):
                v.wait_ge(s_mhs, 16 * i)
                for h in range(NH):
                    hh = NH * i + h
                    v.wait_ge(s_qd, 4 * i + (h // 4) + 1)
                    if hh >= 1:
                        v.wait_ge(s_bss, hh)
                        v.wait_ge(s_btr, 6 * hh)
                    qcol = qf[:, h * 128:(h + 1) * 128]
                    VS.inc(v.tensor_scalar_max(rr3[:, 0:128], qcol, 0.0))
                    VS.inc(v.tensor_scalar(rr3[:, 128:256], qcol, -1.0, 0.0,
                                           op0=ALU.mult, op1=ALU.max))
                    VS.bar(v)
                    for jj in range(3):
                        sft = jj + 1
                        base = jj * 256
                        v.tensor_tensor(fb3[:, base + sft:base + 256],
                                        rr3[:, sft:256], rr3[:, 0:256 - sft],
                                        op=ALU.mult).then_inc(s_bf, 1)
                        v.tensor_tensor(fb3[:, base:base + sft],
                                        rr3[:, 0:sft], rr3[:, 256 - sft:256],
                                        op=ALU.mult).then_inc(s_bf, 1)
                    v.wait_ge(s_bf, 6 * (hh + 1))
                    v.wait_ge(s_bsq, hh + 1)
                    VS.inc(v.reciprocal(brs[:, hh % 2:hh % 2 + 1],
                                        bsqr[:, hh % 2:hh % 2 + 1]))
                    VS.bar(v)
                    v.tensor_scalar(mqb[:], fb3[:], brs[:, hh % 2:hh % 2 + 1], None,
                                    op0=ALU.mult).then_inc(s_bmk, 1)
                    for fc in range(NFC):
                        n = 6 * hh + fc
                        v.wait_ge(s_btr, n + 1)
                        v.tensor_copy(mqT[:, fc, :], ptr3[:, n % 2, 0:128]
                                      ).then_inc(s_btrd, 1)
                    v.wait_ge(s_nap, hh + 1)
                    VS.inc(v.tensor_scalar_add(bdna[:, hh % 2:hh % 2 + 1],
                                               pna[:, hh % 3, 128:129], EPS))
                    VS.bar(v)
                    VS.inc(v.reciprocal(bdna[:, hh % 2:hh % 2 + 1],
                                        bdna[:, hh % 2:hh % 2 + 1]))
                    VS.bar(v)
                    v.tensor_scalar(mhb[:, h * 128:(h + 1) * 128],
                                    pna[:, hh % 3, 0:128],
                                    bdna[:, hh % 2:hh % 2 + 1], None,
                                    op0=ALU.mult).then_inc(s_nad, 1)

        @block.sync
        def _(sy):
            for i in range(NT):
                sy.wait_ge(s_nad, NH * (i + 1))
                sy.dma_start(out=mh_sp[i], in_=mhb[:]).then_inc(s_mhs, 16)

        sB1.close()
        sAR.close()
        sA2.close()
        _barrier(PH["B2"], {
            "gpsimd": [(s_wql, 16), (s_xT3l, 16 * NT)],
            "tensor": [(s_nap, NH * NT), (s_qp, 4 * NT), (s_btr, 6 * NH * NT)],
            "scalar": [(s_bsq, NH * NT), (s_qd, 4 * NT), (s_bss, NH * NT)],
            "vector": [(s_nad, NH * NT), (s_bmk, NH * NT), (s_btrd, 6 * NH * NT),
                       (s_bf, 6 * NH * NT), (VS.s, VS.n)],
            "sync": [(s_mhs, 16 * NT)],
        })

        sB2 = ExitStack()
        wg_sb = sB2.enter_context(nc.sbuf_tensor("wg_sb", [128, NDC, D], BF16))
        xf2 = sB2.enter_context(nc.sbuf_tensor("xf2", [128, 2, D], F32))
        xT4 = sB2.enter_context(nc.sbuf_tensor("xT4", [128, NDC, 128], BF16))
        mh2 = sB2.enter_context(nc.sbuf_tensor("mh2", [128, D], F32))
        gtf = sB2.enter_context(nc.sbuf_tensor("gtf", [128, D], F32))
        res2 = sB2.enter_context(nc.sbuf_tensor("res2", [128, D], F32))
        sqs2 = sB2.enter_context(nc.sbuf_tensor("sqs2", [128, D], F32))
        x1b2 = sB2.enter_context(nc.sbuf_tensor("x1b2", [128, D], F32))
        ln2 = sB2.enter_context(nc.sbuf_tensor("ln2", [128, 4], F32))
        pg = sB2.enter_context(nc.psum_tensor("pg", [128, 2, 512], F32))
        R1 = 6

        @block.gpsimd
        def _(g):
            g.dma_start(out=wg_sb[:], in_=wgT.rearrange("(dc p) o -> p dc o", p=128)
                        ).then_inc(s_wgl, 16)
            for i in range(NT):
                if i >= 1:
                    g.wait_ge(s_gp, 4 * i)
                    g.wait_ge(s_r1, R1 * (i - 1) + 1)
                g.dma_start(out=xT4[:], in_=xT_sp[i].rearrange(
                    "p (dc t) -> p dc t", dc=NDC)).then_inc(s_xT4l, 16)
                g.dma_start(out=mh2[:], in_=mh_sp[i]).then_inc(s_mh2l, 16)

        @block.sync
        def _(sy):
            for i in range(NT):
                if i >= 2:
                    sy.wait_ge(s_r1, R1 * (i - 2) + 1)
                sy.dma_start(out=xf2[:, i % 2], in_=x_in[i * 128:(i + 1) * 128, :]
                             ).then_inc(s_xf2a if i % 2 == 0 else s_xf2b, 16)
                sy.wait_ge(s_r1, R1 * (i + 1))
                sy.dma_start(out=x1_sp[i], in_=x1b2[:]).then_inc(s_x1s, 16)

        @block.tensor
        def _(t):
            for i in range(NT):
                t.wait_ge(s_wgl, 16)
                t.wait_ge(s_xT4l, 16 * (i + 1))
                for sl in range(4):
                    m = 4 * i + sl
                    if m >= 2:
                        t.wait_ge(s_gd, m - 1)
                    for dc in range(NDC):
                        t.matmul(pg[:, m % 2], xT4[:, dc, :],
                                 wg_sb[:, dc, sl * 512:(sl + 1) * 512],
                                 start=(dc == 0), stop=False)
                    t.matmul(pg[:, m % 2], ones[:, 0:128],
                             bgrow[:, sl * 512:(sl + 1) * 512],
                             start=False, stop=True).then_inc(s_gp, 1)

        @block.scalar
        def _(s):
            for i in range(NT):
                for sl in range(4):
                    m = 4 * i + sl
                    s.wait_ge(s_gp, m + 1)
                    if i >= 1:
                        s.wait_ge(s_r1, R1 * (i - 1) + 1)
                    s.activation(gtf[:, sl * 512:(sl + 1) * 512], pg[:, m % 2],
                                 AF.Sigmoid).then_inc(s_gd, 1)
                s.wait_ge(s_r1, R1 * i + 4)
                s.activation(sqs2[:], res2[:], AF.Square,
                             accum_out=ln2[:, 1:2]).then_inc(s_sq2, 1)
                s.wait_ge(s_r1, R1 * i + 5)
                s.activation(ln2[:, 3:4], ln2[:, 2:3], AF.Sqrt).then_inc(s_sqr, 1)

        @block.vector
        def _(v):
            for i in range(NT):
                v.wait_ge(s_gd, 4 * (i + 1))
                v.wait_ge(s_mh2l, 16 * (i + 1))
                v.wait_ge(s_xf2a if i % 2 == 0 else s_xf2b, 16 * (i // 2 + 1))
                if i >= 1:
                    v.wait_ge(s_sq2, i)
                    v.wait_ge(s_x1s, 16 * i)
                VS.inc(v.tensor_tensor(res2[:], gtf[:], mh2[:], op=ALU.mult))
                VS.bar(v)
                v.scalar_tensor_tensor(res2[:], xf2[:, i % 2], 2.0, res2[:],
                                       op0=ALU.mult, op1=ALU.add).then_inc(s_r1, 1)
                v.wait_ge(s_r1, 6 * i + 1)
                v.reduce_sum(ln2[:, 0:1], res2[:], axis=AX.X).then_inc(s_r1, 1)
                v.wait_ge(s_r1, 6 * i + 2)
                v.tensor_scalar_mul(ln2[:, 0:1], ln2[:, 0:1], -1.0 / D).then_inc(s_r1, 1)
                v.wait_ge(s_r1, 6 * i + 3)
                v.tensor_scalar(res2[:], res2[:], ln2[:, 0:1], None,
                                op0=ALU.add).then_inc(s_r1, 1)
                v.wait_ge(s_sq2, i + 1)
                v.tensor_scalar(ln2[:, 2:3], ln2[:, 1:2], 1.0 / D, 1e-5,
                                op0=ALU.mult, op1=ALU.add).then_inc(s_r1, 1)
                v.wait_ge(s_r1, 6 * i + 5)
                v.wait_ge(s_sqr, i + 1)
                VS.inc(v.reciprocal(ln2[:, 3:4], ln2[:, 3:4]))
                VS.bar(v)
                v.tensor_scalar(x1b2[:], res2[:], ln2[:, 3:4], None,
                                op0=ALU.mult).then_inc(s_r1, 1)

        sB2.close()
        attn_ctx.close()
        _barrier(PH["C"], {
            "gpsimd": [(s_wgl, 16), (s_xT4l, 16 * NT), (s_mh2l, 16 * NT)],
            "tensor": [(s_gp, 4 * NT)],
            "scalar": [(s_sqr, NT), (s_gd, 4 * NT), (s_sq2, NT)],
            "vector": [(s_r1, 6 * NT), (VS.s, VS.n)],
            "sync": [(s_x1s, 16 * NT), (s_xf2a, 16 * 8), (s_xf2b, 16 * 8)],
        })

        sC = ExitStack()
        accC = sC.enter_context(nc.sbuf_tensor("accC", [128, NTH, D], F32))
        x1T = sC.enter_context(nc.sbuf_tensor("x1T", [128, NDC, HALF], BF16))
        y1fm = sC.enter_context(nc.sbuf_tensor("y1fm", [128, NHC, HALF], BF16))
        w2sb = sC.enter_context(nc.sbuf_tensor("w2sb", [128, NHC, D], BF16))
        w1ch = sC.enter_context(nc.sbuf_tensor("w1ch", [128, 3, NDC * 128], BF16))
        gsb = sC.enter_context(nc.sbuf_tensor("gsb", [128, NTH, NE], F32))
        gbf = sC.enter_context(nc.sbuf_tensor("gbf", [128, NE], BF16))
        gT = sC.enter_context(nc.sbuf_tensor("gT", [NE, HALF], BF16))
        x1bb = sC.enter_context(nc.sbuf_tensor("x1bb", [128, D], BF16))
        expv = sC.enter_context(nc.sbuf_tensor("expv", [128, 32], F32))
        wrkC = sC.enter_context(nc.sbuf_tensor("wrkC", [128, 32], F32))
        mskC = sC.enter_context(nc.sbuf_tensor("mskC", [128, 32], F32))
        smC = sC.enter_context(nc.sbuf_tensor("smC", [128, 4], F32))
        lnC = sC.enter_context(nc.sbuf_tensor("lnC", [128, 4], F32))
        x1f = sC.enter_context(nc.sbuf_tensor("x1f", [128, D], F32))
        resC = sC.enter_context(nc.sbuf_tensor("resC", [128, D], F32))
        ptrC = sC.enter_context(nc.psum_tensor("ptrC", [128, 2, 1024], BF16))
        py1 = sC.enter_context(nc.psum_tensor("py1", [128, 2, 512], F32))
        py2 = sC.enter_context(nc.psum_tensor("py2", [128, 4, 512], F32))

        Y1G = 2 * NHC
        Y2G = NTH * 4
        EXP_Y2 = (NE + 1) * Y2G
        CTR = 17
        EP3 = 7

        for half in range(2):
            t0 = half * NTH
            b_cl = half * NTH * 16
            b_ctr = half * NTH * CTR
            b_lg = half * NTH
            b_w1 = half * NE * NHC
            b_w2 = half * NE * 16
            b_y1 = half * NE * Y1G
            b_y2 = half * EXP_Y2
            b_ep = half * NTH
            b_os = half * NTH * 16

            @block.gpsimd
            def _(g, half=half, t0=t0, b_cl=b_cl, b_ctr=b_ctr, b_w1=b_w1,
                  b_w2=b_w2, b_y2=b_y2):
                for tt in range(NTH):
                    glob = half * NTH + tt
                    if glob >= 1:
                        g.wait_ge(s_ctr, CTR * glob - 1)
                    g.dma_start(out=x1bb[:], in_=x1_sp[t0 + tt]).then_inc(s_cl, 16)
                for e in range(NE):
                    if half == 1 or e >= 1:
                        g.wait_ge(s_y2p, b_y2 + Y2G * e)
                    for hc in range(NHC):
                        n = b_w1 + e * NHC + hc
                        if n >= 3:
                            g.wait_ge(s_y1p, 2 * (n - 2))
                        g.dma_start(out=w1ch[:, n % 3], in_=w1h[e, hc]
                                    ).then_inc(s_w1s[n % 3], 16)
                        g.dma_start(out=w2sb[:, hc, :],
                                    in_=w2h[e][hc * 128:(hc + 1) * 128, :]
                                    ).then_inc(s_w2, 16)

            @block.tensor
            def _(t, half=half, b_cl=b_cl, b_ctr=b_ctr, b_lg=b_lg, b_w1=b_w1,
                  b_w2=b_w2, b_y1=b_y1, b_y2=b_y2):
                for tt in range(NTH):
                    glob = half * NTH + tt
                    t.wait_ge(s_cl, b_cl + 16 * (tt + 1))
                    if half == 1 and tt == 0:
                        t.wait_ge(s_y1p, NE * Y1G)
                        t.wait_ge(s_y1d, NE * Y1G)
                    for dc in range(NDC):
                        n = b_ctr + CTR * tt + dc
                        if n >= 2:
                            t.wait_ge(s_ctrd, n - 1)
                        t.transpose(ptrC[:, n % 2, 0:128],
                                    x1bb[:, dc * 128:(dc + 1) * 128],
                                    ident[:]).then_inc(s_ctr, 1)
                    t.wait_ge(s_ctrd, b_ctr + CTR * tt + 16)
                    if glob >= 2:
                        t.wait_ge(s_sm2, glob - 1)
                    for dc in range(NDC):
                        t.matmul(py1[:, tt % 2, 0:32],
                                 x1T[:, dc, tt * 128:(tt + 1) * 128],
                                 rwsb[:, dc, :], start=(dc == 0), stop=False)
                    t.matmul(py1[:, tt % 2, 0:32], ones[:, 0:128], rbrrow[:],
                             start=False, stop=True).then_inc(s_lgp, 1)
                    t.wait_ge(s_lgd, b_lg + tt + 1)
                    n = b_ctr + CTR * tt + 16
                    if n >= 2:
                        t.wait_ge(s_ctrd, n - 1)
                    t.transpose(ptrC[0:NE, n % 2, 0:128], gbf[:], ident[:]
                                ).then_inc(s_ctr, 1)
                t.wait_ge(s_sm2, (half + 1) * NTH)
                for e in range(NE):
                    for hc in range(NHC):
                        n = b_w1 + e * NHC + hc
                        t.wait_ge(s_w1s[n % 3], 16 * (n // 3 + 1))
                        for tg in range(2):
                            k = 2 * n + tg
                            if k >= 2:
                                t.wait_ge(s_y1d, k - 1)
                            for dc in range(NDC):
                                mm = t.matmul(
                                    py1[:, k % 2],
                                    w1ch[:, n % 3, dc * 128:(dc + 1) * 128],
                                    x1T[:, dc, tg * 512:(tg + 1) * 512],
                                    start=(dc == 0), stop=(dc == NDC - 1))
                            mm.then_inc(s_y1p, 1)
                    t.wait_ge(s_w2, NHC * (b_w2 + 16 * (e + 1)))
                    t.wait_ge(s_y1d, b_y1 + Y1G * (e + 1))
                    for tc in range(NTH):
                        for ds in range(4):
                            m = b_y2 + e * Y2G + tc * 4 + ds
                            if m >= 4:
                                t.wait_ge(s_y2d, m - 3)
                            for hc in range(NHC):
                                mm = t.matmul(py2[:, m % 4],
                                              y1fm[:, hc, tc * 128:(tc + 1) * 128],
                                              w2sb[:, hc, ds * 512:(ds + 1) * 512],
                                              start=(hc == 0), stop=(hc == NHC - 1))
                            mm.then_inc(s_y2p, 1)
                t.wait_ge(s_ctrd, (half + 1) * NTH * CTR)
                for tc in range(NTH):
                    for ds in range(4):
                        m = b_y2 + NE * Y2G + tc * 4 + ds
                        t.wait_ge(s_y2d, m - 3)
                        t.matmul(py2[:, m % 4], gT[:, tc * 128:(tc + 1) * 128],
                                 b2raw[:, ds * 512:(ds + 1) * 512],
                                 start=True, stop=True).then_inc(s_y2p, 1)

            @block.scalar
            def _(s, half=half, b_lg=b_lg, b_y1=b_y1, b_y2=b_y2, b_ep=b_ep):
                for tt in range(NTH):
                    glob = half * NTH + tt
                    s.wait_ge(s_sm1, glob + 1)
                    if glob >= 1:
                        s.wait_ge(s_lgd, glob)
                    s.activation(expv[:], py1[:, tt % 2, 0:32], AF.Exp,
                                 bias=smC[:, 0:1]).then_inc(s_sm2, 1)
                for e in range(NE):
                    for hc in range(NHC):
                        for tg in range(2):
                            k = b_y1 + e * Y1G + hc * 2 + tg
                            s.wait_ge(s_y1p, k + 1)
                            if e >= 1:
                                s.wait_ge(s_y2p, b_y2 + Y2G * e)
                            s.activation(y1fm[:, hc, tg * 512:(tg + 1) * 512],
                                         py1[:, k % 2], AF.Relu,
                                         bias=b1col[:, hc, e:e + 1]).then_inc(s_y1d, 1)
                for tt in range(NTH):
                    glob = half * NTH + tt
                    s.wait_ge(s_ep3, EP3 * glob + 4)
                    s.activation(x1f[:], resC[:], AF.Square,
                                 accum_out=lnC[:, 1:2]).then_inc(s_ep2, 1)
                    s.wait_ge(s_ep3, EP3 * glob + 5)
                    s.activation(lnC[:, 3:4], lnC[:, 2:3], AF.Sqrt).then_inc(s_ep2, 1)

            @block.vector
            def _(v, half=half, b_ctr=b_ctr, b_lg=b_lg, b_y2=b_y2, b_ep=b_ep,
                  b_y1=b_y1):
                VS.inc(v.memset(gsb[:], 0.5))
                VS.bar(v)
                for tt in range(NTH):
                    glob = half * NTH + tt
                    if half == 1 and tt == 0:
                        v.wait_ge(s_y1p, NE * Y1G)
                    for dc in range(NDC):
                        n = b_ctr + CTR * tt + dc
                        v.wait_ge(s_ctr, n + 1)
                        v.tensor_copy(x1T[:, dc, tt * 128:(tt + 1) * 128],
                                      ptrC[:, n % 2, 0:128]).then_inc(s_ctrd, 1)
                    v.wait_ge(s_lgp, glob + 1)
                    VS.inc(v.reduce_max(smC[:, 0:1], py1[:, tt % 2, 0:32], axis=AX.X))
                    VS.bar(v)
                    v.tensor_scalar_mul(smC[:, 0:1], smC[:, 0:1], -1.0
                                        ).then_inc(s_sm1, 1)
                    v.wait_ge(s_sm2, glob + 1)
                    VS.inc(v.tensor_copy(wrkC[:], expv[:]))
                    VS.bar(v)
                    for r in range(5):
                        VS.inc(v.reduce_max(smC[:, 1:2], wrkC[:], axis=AX.X))
                        VS.bar(v)
                        VS.inc(v.tensor_scalar(mskC[:], wrkC[:], smC[:, 1:2], None,
                                               op0=ALU.is_equal))
                        VS.bar(v)
                        VS.inc(v.scalar_tensor_tensor(wrkC[:], mskC[:], -BIG, wrkC[:],
                                                      op0=ALU.mult, op1=ALU.add))
                        VS.bar(v)
                    VS.inc(v.reduce_max(smC[:, 1:2], wrkC[:], axis=AX.X))
                    VS.bar(v)
                    VS.inc(v.tensor_scalar(mskC[:], expv[:], smC[:, 1:2], None,
                                           op0=ALU.is_ge))
                    VS.bar(v)
                    VS.inc(v.tensor_tensor(mskC[:], mskC[:], expv[:], op=ALU.mult))
                    VS.bar(v)
                    VS.inc(v.reduce_sum(smC[:, 2:3], mskC[:], axis=AX.X))
                    VS.bar(v)
                    VS.inc(v.reciprocal(smC[:, 2:3], smC[:, 2:3]))
                    VS.bar(v)
                    VS.inc(v.tensor_scalar(gsb[:, tt, 0:32], mskC[:], smC[:, 2:3],
                                           None, op0=ALU.mult))
                    VS.bar(v)
                    if glob >= 1:
                        v.wait_ge(s_ctr, CTR * glob)
                    v.tensor_copy(gbf[:], gsb[:, tt, :]).then_inc(s_lgd, 1)
                    if half == 1 and tt == 0:
                        v.wait_ge(s_y2p, EXP_Y2)
                    n = b_ctr + CTR * tt + 16
                    v.wait_ge(s_ctr, n + 1)
                    v.tensor_copy(gT[:, tt * 128:(tt + 1) * 128],
                                  ptrC[0:NE, n % 2, 0:128]).then_inc(s_ctrd, 1)
                for e in range(NE):
                    for tc in range(NTH):
                        for ds in range(4):
                            m = b_y2 + e * Y2G + tc * 4 + ds
                            v.wait_ge(s_y2p, m + 1)
                            if e == 0:
                                v.tensor_scalar(accC[:, tc, ds * 512:(ds + 1) * 512],
                                                py2[:, m % 4], gsb[:, tc, 0:1], None,
                                                op0=ALU.mult).then_inc(s_y2d, 1)
                            else:
                                ge = gsb[:, tc, e:e + 1]
                                v.scalar_tensor_tensor(
                                    accC[:, tc, ds * 512:(ds + 1) * 512],
                                    py2[:, m % 4], ge,
                                    accC[:, tc, ds * 512:(ds + 1) * 512],
                                    op0=ALU.mult, op1=ALU.add).then_inc(s_y2d, 1)
                for tc in range(NTH):
                    for ds in range(4):
                        m = b_y2 + NE * Y2G + tc * 4 + ds
                        v.wait_ge(s_y2p, m + 1)
                        v.tensor_tensor(accC[:, tc, ds * 512:(ds + 1) * 512],
                                        accC[:, tc, ds * 512:(ds + 1) * 512],
                                        py2[:, m % 4], op=ALU.add).then_inc(s_y2d, 1)
                v.wait_ge(s_y2d, b_y2 + EXP_Y2)
                for tt in range(NTH):
                    glob = half * NTH + tt
                    v.wait_ge(s_ep, 16 * (glob + 1))
                    if glob >= 1:
                        v.wait_ge(s_ep2, 2 * glob)
                        pg_ = glob - 1
                        v.wait_ge(s_osa if pg_ % 2 == 0 else s_osb,
                                  16 * (pg_ // 2 + 1))
                    v.tensor_tensor(resC[:], x1f[:], accC[:, tt], op=ALU.add
                                    ).then_inc(s_ep3, 1)
                    v.wait_ge(s_ep3, EP3 * glob + 1)
                    v.reduce_sum(lnC[:, 0:1], resC[:], axis=AX.X).then_inc(s_ep3, 1)
                    v.wait_ge(s_ep3, EP3 * glob + 2)
                    v.tensor_scalar_mul(lnC[:, 0:1], lnC[:, 0:1], -1.0 / D
                                        ).then_inc(s_ep3, 1)
                    v.wait_ge(s_ep3, EP3 * glob + 3)
                    v.tensor_scalar(resC[:], resC[:], lnC[:, 0:1], None,
                                    op0=ALU.add).then_inc(s_ep3, 1)
                    v.wait_ge(s_ep2, 2 * glob + 1)
                    v.tensor_scalar(lnC[:, 2:3], lnC[:, 1:2], 1.0 / D, 1e-5,
                                    op0=ALU.mult, op1=ALU.add).then_inc(s_ep3, 1)
                    v.wait_ge(s_ep3, EP3 * glob + 5)
                    v.wait_ge(s_ep2, 2 * glob + 2)
                    v.reciprocal(lnC[:, 3:4], lnC[:, 3:4]).then_inc(s_ep3, 1)
                    v.wait_ge(s_ep3, EP3 * glob + 6)
                    v.tensor_scalar(x1f[:], resC[:], lnC[:, 3:4], None,
                                    op0=ALU.mult).then_inc(s_ep3, 1)

            @block.sync
            def _(sy, half=half, t0=t0):
                for tt in range(NTH):
                    glob = half * NTH + tt
                    if glob >= 1:
                        pg_ = glob - 1
                        sy.wait_ge(s_osa if pg_ % 2 == 0 else s_osb,
                                   16 * (pg_ // 2 + 1))
                    sy.dma_start(out=x1f[:], in_=x1_sp[t0 + tt]).then_inc(s_ep, 16)
                    sy.wait_ge(s_ep3, EP3 * (glob + 1))
                    sy.dma_start(out=y_out[(t0 + tt) * 128:(t0 + tt + 1) * 128, :],
                                 in_=x1f[:]).then_inc(
                                     s_osa if glob % 2 == 0 else s_osb, 16)

        sC.close()
        _tap = os.environ.get("KDBG", "")
        if _tap == "mkb":
            s_dbg = sem("s_dbg")

            @block.gpsimd
            def _(g):
                g.wait_ge(s_osa, 16 * (NT // 2))
                g.wait_ge(s_osb, 16 * (NT // 2))
                for i in range(NT):
                    g.dma_start(out=y_out[i * 128:(i + 1) * 128, :],
                                in_=mkb_sp[i][:, 0:2048]).then_inc(s_dbg, 16)
                g.wait_ge(s_dbg, 16 * NT)
        elif _tap:
            _src = {"x1": x1_sp, "mh": mh_sp, "mv": mv_sp}[_tap]
            s_dbg = sem("s_dbg")

            @block.sync
            def _(sy):
                sy.wait_ge(s_osa, 16 * (NT // 2))
                sy.wait_ge(s_osb, 16 * (NT // 2))
                for i in range(NT):
                    sy.dma_start(out=y_out[i * 128:(i + 1) * 128, :], in_=_src[i]
                                 ).then_inc(s_dbg, 16)

            @block.gpsimd
            def _(g):
                g.wait_ge(s_dbg, 16 * NT)
        else:
            @block.gpsimd
            def _(g):
                g.wait_ge(s_osa, 16 * (NT // 2))
                g.wait_ge(s_osb, 16 * (NT // 2))

    return nc


def kernel(**inputs):
    inp = {k: np.asarray(v) for k, v in inputs.items()}
    x = np.ascontiguousarray(inp["L0_attn_hidden_states"].reshape(TOK * NCORES, D))
    w1 = np.concatenate([inp["L0_ffn_rw1"], inp["L0_ffn_sw1"]], axis=0)
    w2 = np.concatenate([inp["L0_ffn_rw2"], inp["L0_ffn_sw2"]], axis=0)
    b1 = np.concatenate([inp["L0_ffn_rb1"], inp["L0_ffn_sb1"]], axis=0)
    b2 = np.concatenate([inp["L0_ffn_rb2"], inp["L0_ffn_sb2"]], axis=0)
    w1h = np.ascontiguousarray(
        w1.reshape(NE, NDC, 128, NHC, 128).transpose(0, 3, 2, 1, 4)
        .reshape(NE, NHC, 128, NDC * 128))
    common = {
        "wkT": np.ascontiguousarray(inp["L0_attn_Wk"].T),
        "wvT": np.ascontiguousarray(inp["L0_attn_Wv"].T),
        "wqT": np.ascontiguousarray(inp["L0_attn_Wq"].T),
        "wgT": np.ascontiguousarray(inp["L0_attn_Wg"].T),
        "wmbT": np.ascontiguousarray(inp["L0_attn_Wmb"].T),
        "bk": inp["L0_attn_bk"], "bv": inp["L0_attn_bv"],
        "bq": inp["L0_attn_bq"], "bg": inp["L0_attn_bg"],
        "bmb": inp["L0_attn_bmb"],
        "mem": inp["L0_attn_memory"], "nrm": inp["L0_attn_norm"],
        "w1h": w1h, "w2h": w2, "b1h": b1, "b2h": b2,
        "rwr": inp["L0_ffn_router_w"], "rbr": inp["L0_ffn_router_b"],
    }
    in_maps = [dict(common, x=np.ascontiguousarray(x[c * TOK:(c + 1) * TOK]))
               for c in range(NCORES)]
    nc = build_nc()
    res = run_bass_kernel_spmd(nc, in_maps, core_ids=list(range(NCORES)))
    out = np.concatenate([res.results[c]["y"] for c in range(NCORES)], axis=0)
    return out.reshape(4, 4096, D)
